# version 02
# baseline (speedup 1.0000x reference)
import math
import contextlib
import numpy as np
import ml_dtypes
import concourse.bass as bass
import concourse.mybir as mybir
from concourse.bass_utils import run_bass_kernel_spmd

F32 = mybir.dt.float32
BF16 = mybir.dt.bfloat16
AF = mybir.ActivationFunctionType
OP = mybir.AluOpType
AXX = mybir.AxisListType.X

M1 = 12582912.0
M2 = 6755399441055744.0
HNORM = float(np.float32(1.0 / np.sqrt(8192.0)))
ISQ = 1.0 / math.sqrt(128.0)
NEGC = -10.0
EPS = 1e-5
INV2048 = 1.0 / 2048.0
INV128 = 1.0 / 128.0

TPC = 1024
D = 2048
I8K = 8192
QT = 256
NQ = TPC // QT
NSUB = QT // 128

ENGINES = ("sync", "tensor", "vector", "scalar", "gpsimd")


class Sem:
    def __init__(self, handle):
        self.h = handle
        self.v = 0


class Sched:
    def __init__(self):
        self.ops = {e: [] for e in ENGINES}

    def add(self, eng, fn, waits=(), incs=()):
        wl = [(s.h, int(v)) for (s, v) in waits]
        il = []
        for (s, amt) in incs:
            s.v += amt
            il.append((s.h, amt))
        self.ops[eng].append((wl, fn, il))

    def emit(self, block):
        def runner(eng):
            def run(E):
                for (wl, fn, il) in self.ops[eng]:
                    for (h, v) in wl:
                        E.wait_ge(h, v)
                    ins = fn(E)
                    for (h, amt) in il:
                        ins = ins.then_inc(h, amt)
            return run
        block.sync(runner("sync"))
        block.tensor(runner("tensor"))
        block.vector(runner("vector"))
        block.scalar(runner("scalar"))
        block.gpsimd(runner("gpsimd"))


def build_program(debug=False):
    nc = bass.Bass("TRN2", target_bir_lowering=False)

    d_x = nc.dram_tensor("x", [TPC, D], F32, kind="ExternalInput").ap()
    d_wg = nc.dram_tensor("wg", [I8K, D], F32, kind="ExternalInput").ap()
    d_wu = nc.dram_tensor("wu", [I8K, D], F32, kind="ExternalInput").ap()
    d_wd = nc.dram_tensor("wd", [D, I8K], F32, kind="ExternalInput").ap()
    d_wqkv = nc.dram_tensor("wqkv", [D, 6144], F32, kind="ExternalInput").ap()
    d_wout = nc.dram_tensor("wout", [D, D], F32, kind="ExternalInput").ap()
    d_bqf = nc.dram_tensor("bq_f", [128, 48], F32, kind="ExternalInput").ap()
    d_bout = nc.dram_tensor("bout_r", [128, D], F32, kind="ExternalInput").ap()
    d_idf = nc.dram_tensor("ident_f", [128, 128], F32, kind="ExternalInput").ap()
    d_idb = nc.dram_tensor("ident_b", [128, 128], BF16, kind="ExternalInput").ap()
    d_h128 = nc.dram_tensor("h128", [128, 128], F32, kind="ExternalInput").ap()
    d_out = nc.dram_tensor("out", [TPC, D], F32, kind="ExternalOutput").ap()

    d_rs = nc.dram_tensor("rs", [TPC, D], F32).ap()
    d_wgt = nc.dram_tensor("wgt_s", [64, D, 128], BF16).ap()
    d_wut = nc.dram_tensor("wut_s", [64, D, 128], BF16).ap()
    d_wdt = nc.dram_tensor("wdt_s", [64, 128, D], BF16).ap()
    d_qs = nc.dram_tensor("qs_s", [TPC, 6144], F32).ap()
    d_wo2 = nc.dram_tensor("wo2_s", [2, D, D], BF16).ap()

    es = contextlib.ExitStack()
    with es:
        def sbuf(name, shape, dt):
            return es.enter_context(nc.sbuf_tensor(name, shape, dt))

        def psum(name, shape, dt):
            return es.enter_context(nc.psum_tensor(name, shape, dt))

        AR = sbuf("arena", [128, 20480], F32)
        xt = sbuf("xt", [128, 2 * D], F32)
        xn = sbuf("xn", [128, D], F32)
        tq1 = sbuf("tq1", [128, D], F32)
        tq2 = sbuf("tq2", [128, D], F32)
        xqb = sbuf("xqb", [128, D], BF16)
        xqT = sbuf("xqT", [128, 4096], BF16)
        wsl = sbuf("wslab", [128, 8192], BF16)
        wdsl = sbuf("wdslab", [128, 1024], BF16)
        ga = sbuf("ga", [128, 512], F32)
        gb = sbuf("gb", [128, 512], F32)
        gc = sbuf("gc", [128, 512], F32)
        gd = sbuf("gd", [128, 512], F32)
        sbt = sbuf("sbt", [128, 512], F32)
        sbt2 = sbuf("sbt2", [128, 512], F32)
        rowt = sbuf("rowt", [1, 512], F32)
        onesr = sbuf("onesr", [1, 128], F32)
        stats = sbuf("stats", [128, 64], F32)
        consts = sbuf("consts", [128, 16], F32)
        idf = sbuf("idf", [128, 128], F32)
        idb = sbuf("idb", [128, 128], BF16)
        h128 = sbuf("h128t", [128, 128], F32)
        bqf = sbuf("bqf", [128, 48], F32)
        boutr = sbuf("boutr", [128, D], F32)

        P = [psum("P%d" % i, [128, 512], F32) for i in range(6)]
        PBa = psum("PBa", [128, 1024], BF16)
        PBb = psum("PBb", [128, 1024], BF16)

        sD = Sem(es.enter_context(nc.semaphore("sD")))
        sG = Sem(es.enter_context(nc.semaphore("sG")))
        sT = Sem(es.enter_context(nc.semaphore("sT")))
        sV = Sem(es.enter_context(nc.semaphore("sV")))
        sS = Sem(es.enter_context(nc.semaphore("sS")))
        sC = Sem(es.enter_context(nc.semaphore("sC")))
        sDab = [Sem(es.enter_context(nc.semaphore("sDa"))),
                Sem(es.enter_context(nc.semaphore("sDb")))]
        sDcd = [Sem(es.enter_context(nc.semaphore("sDc"))),
                Sem(es.enter_context(nc.semaphore("sDd")))]
        sDef = [Sem(es.enter_context(nc.semaphore("sDe"))),
                Sem(es.enter_context(nc.semaphore("sDf")))]

        sch = Sched()

        def op(eng, fn, waits=(), incs=()):
            sch.add(eng, fn, waits=waits, incs=incs)

        def W(sem, v=None):
            return (sem, sem.v if v is None else v)

        def dma(eng, out, in_, waits=(), sem=None):
            if sem is None:
                sem = sD if eng == "sync" else sG
            op(eng, lambda E, o=out, i=in_: E.dma_start(out=o, in_=i),
               waits=waits, incs=[(sem, 16)])

        def vop(fn, waits=(), inc=False):
            op("vector", fn, waits=waits, incs=([(sV, 1)] if inc else ()))

        def sop(fn, waits=(), inc=True):
            op("scalar", fn, waits=waits, incs=([(sS, 1)] if inc else ()))

        def vdrain():
            op("vector", lambda E: E.drain())

        def sdrain():
            op("scalar", lambda E: E.drain())

        def arf(c0, n):
            return AR[:, c0:c0 + n]

        def arb(c0, nbf):
            return AR[:, c0:c0 + nbf // 2].bitcast(BF16)

        G1M_AP = consts[:, 1:2]
        NEGC_AP = consts[:, 2:3]
        SWG_AP = consts[:, 3:4]
        SWU_AP = consts[:, 4:5]
        SWD_AP = consts[:, 5:6]
        TH_AP = [consts[:, 6:7], consts[:, 7:8], consts[:, 8:9]]
        NTH_AP = [consts[:, 9:10], consts[:, 10:11], consts[:, 11:12]]

        dma("sync", idf[:], d_idf)
        dma("sync", idb[:], d_idb)
        dma("sync", h128[:], d_h128)
        dma("sync", bqf[:], d_bqf)
        dma("sync", boutr[:], d_bout)

        vop(lambda E: E.memset(consts[:, 13:16], 0.0))
        vop(lambda E: E.memset(consts[:, 2:3], NEGC))
        vop(lambda E: E.memset(consts[:, 12:13], 1.0))
        vop(lambda E: E.memset(onesr[:, :], 1.0))
        vdrain()
        vop(lambda E: E.tensor_scalar(consts[:, 0:1], consts[:, 12:13],
                                      M2, M2, OP.add, OP.subtract))
        vdrain()
        vop(lambda E: E.tensor_scalar(consts[:, 1:2], consts[:, 0:1],
                                      -1.0, 1.0, OP.mult, OP.add), inc=True)
        vdrain()

        for mi in range(3):
            vop(lambda E, m=mi: E.memset(stats[:, 32 + m:33 + m], 0.0),
                inc=(mi == 2))
        vdrain()

        def mean_chunks(mi):
            if mi == 0:
                return [d_wg[p * 128:(p + 1) * 128, :] for p in range(64)]
            if mi == 1:
                return [d_wu[p * 128:(p + 1) * 128, :] for p in range(64)]
            return [d_wd[rt * 128:(rt + 1) * 128, q2 * D:(q2 + 1) * D]
                    for rt in range(16) for q2 in range(4)]

        for mi in range(3):
            chunks = mean_chunks(mi)
            for ci, ch in enumerate(chunks):
                dma("sync", xt[:, 0:D], ch, waits=[W(sS)])
                sop(lambda E, c=ci % 8: E.activation(
                    tq1[:], xt[:, 0:D], AF.Abs,
                    accum_out=stats[:, c:c + 1]), waits=[W(sD), W(sV)])
                sdrain()
                if ci % 8 == 7:
                    vop(lambda E: E.reduce_sum(stats[:, 8:9],
                                               stats[:, 0:8], axis=AXX),
                        waits=[W(sS)])
                    vdrain()
                    vop(lambda E, m=mi: E.tensor_tensor(
                        stats[:, 9:10], stats[:, 32 + m:33 + m],
                        stats[:, 8:9], OP.add))
                    vdrain()
                    vop(lambda E, m=mi: E.tensor_copy(
                        stats[:, 32 + m:33 + m], stats[:, 9:10]),
                        inc=True)
                    vdrain()
        vop(lambda E: E.memset(stats[:, 35:36], 1.0), inc=True)
        vdrain()
        for mi in range(3):
            op("tensor",
               lambda E, m=mi: E.matmul(P[5][0:1, m:m + 1],
                                        stats[:, 32 + m:33 + m],
                                        stats[:, 35:36], start=True,
                                        stop=True),
               waits=([W(sV)] if mi == 0 else ()),
               incs=([(sT, 1)] if mi == 2 else ()))
        vop(lambda E: E.tensor_copy(rowt[0:1, 0:3], P[5][0:1, 0:3]),
            waits=[W(sT)], inc=True)
        vdrain()
        op("tensor",
           lambda E: E.matmul(P[5][:, 0:3], onesr[0:1, :], rowt[0:1, 0:3],
                              start=True, stop=True),
           waits=[W(sV)], incs=[(sT, 1)])
        for mi in range(3):
            vop(lambda E, m=mi: E.tensor_scalar(
                consts[:, 3 + m:4 + m], P[5][:, m:m + 1],
                1.0 / 16777216.0, EPS, OP.mult, OP.max),
                waits=([W(sT)] if mi == 0 else ()))
        vdrain()
        for mi in range(3):
            vop(lambda E, m=mi: E.tensor_scalar(
                consts[:, 6 + m:7 + m], consts[:, 3 + m:4 + m],
                0.5, None, OP.mult))
        vdrain()
        for mi in range(3):
            vop(lambda E, m=mi: E.tensor_scalar(
                consts[:, 9 + m:10 + m], consts[:, 6 + m:7 + m],
                -1.0, None, OP.mult), inc=(mi == 2))
        vdrain()

        def quant_wgwu(d_src, d_dst, mi):
            for p in range(64):
                dma("sync", xt[:, 0:D], d_src[p * 128:(p + 1) * 128, :],
                    waits=[W(sV)])
                ldv = sD.v
                vop(lambda E: E.tensor_scalar(
                    tq1[:], xt[:, 0:D], NTH_AP[mi], None, OP.is_le),
                    waits=[W(sD, ldv)])
                vdrain()
                vop(lambda E, mi=mi: E.scalar_tensor_tensor(
                    xqb[:], xt[:, 0:D], TH_AP[mi], tq1[:], OP.is_ge,
                    OP.subtract), inc=True)
                vdrain()
                vv = sV.v
                for hc in range(16):
                    pb = PBa if hc < 8 else PBb
                    off = (hc % 8) * 128
                    op("tensor",
                       lambda E, pb=pb, off=off, hc=hc: E.transpose(
                           pb[:, off:off + 128],
                           xqb[:, hc * 128:(hc + 1) * 128], idb[:]),
                       waits=([W(sV, vv)] if hc == 0 else ()),
                       incs=([(sT, 1)] if hc in (7, 15) else ()))
                tv = sT.v
                vop(lambda E: E.tensor_copy(xqT[:, 0:1024], PBa[:]),
                    waits=[W(sT, tv - 1), W(sG)])
                vop(lambda E: E.tensor_copy(xqT[:, 1024:2048], PBb[:]),
                    waits=[W(sT, tv)], inc=True)
                vdrain()
                dma("gpsimd",
                    d_dst[p].rearrange("(hc hp) i -> hp hc i", hp=128),
                    xqT[:, 0:2048].rearrange("p (hc i) -> p hc i", i=128),
                    waits=[W(sV)])

        quant_wgwu(d_wg, d_wgt, 0)
        quant_wgwu(d_wu, d_wut, 1)

        wdq = arb(0, 8192)
        for rt in range(16):
            for half in range(2):
                for q2 in range(2):
                    c0 = half * 4096 + q2 * D
                    dma("sync", xt[:, q2 * D:(q2 + 1) * D],
                        d_wd[rt * 128:(rt + 1) * 128, c0:c0 + D],
                        waits=([W(sV)] if q2 == 0 else ()))
                ldv = sD.v
                for q2 in range(2):
                    vop(lambda E, q2=q2: E.tensor_scalar(
                        tq1[:], xt[:, q2 * D:(q2 + 1) * D], NTH_AP[2], None,
                        OP.is_le), waits=([W(sD, ldv)] if q2 == 0 else ()))
                    vdrain()
                    vop(lambda E, q2=q2, half=half: E.scalar_tensor_tensor(
                        wdq[:, half * 4096 + q2 * D:
                            half * 4096 + (q2 + 1) * D],
                        xt[:, q2 * D:(q2 + 1) * D], TH_AP[2], tq1[:],
                        OP.is_ge, OP.subtract), inc=(q2 == 1))
                    vdrain()
            vv = sV.v
            for grp in range(4):
                for k in range(16):
                    p = grp * 16 + k
                    pb = PBa if k < 8 else PBb
                    off = (k % 8) * 128
                    op("tensor",
                       lambda E, pb=pb, off=off, p=p: E.transpose(
                           pb[:, off:off + 128],
                           wdq[:, p * 128:(p + 1) * 128], idb[:]),
                       waits=([W(sV, vv)] if (grp == 0 and k == 0)
                              else ([W(sV)] if k == 0 else ())),
                       incs=([(sT, 1)] if k in (7, 15) else ()))
                tv = sT.v
                vop(lambda E: E.tensor_copy(xqT[:, 0:1024], PBa[:]),
                    waits=[W(sT, tv - 1), W(sG)])
                vop(lambda E: E.tensor_copy(xqT[:, 1024:2048], PBb[:]),
                    waits=[W(sT, tv)], inc=True)
                vdrain()
                for k in range(16):
                    p = grp * 16 + k
                    dma("gpsimd",
                        d_wdt[p, :, rt * 128:(rt + 1) * 128],
                        xqT[:, k * 128:(k + 1) * 128],
                        waits=([W(sV)] if k == 0 else ()))

        def ln_stats(xsl, ld_waits):
            vop(lambda E, xsl=xsl: E.reduce_sum(stats[:, 40:41], xsl,
                                                axis=AXX), waits=ld_waits)
            sop(lambda E, xsl=xsl: E.activation(
                tq1[:], xsl, AF.Square, accum_out=stats[:, 41:42]),
                waits=ld_waits)
            sdrain()
            vdrain()
            vop(lambda E: E.tensor_scalar(stats[:, 42:43], stats[:, 40:41],
                                          INV2048, None, OP.mult))
            vop(lambda E: E.tensor_scalar(stats[:, 43:44], stats[:, 41:42],
                                          INV2048, None, OP.mult),
                waits=[W(sS)])
            vdrain()
            vop(lambda E: E.tensor_tensor(stats[:, 44:45], stats[:, 42:43],
                                          stats[:, 42:43], OP.mult))
            vdrain()
            vop(lambda E: E.scalar_tensor_tensor(
                stats[:, 45:46], stats[:, 43:44], EPS, stats[:, 44:45],
                OP.add, OP.subtract), inc=True)
            vdrain()
            sop(lambda E: E.sqrt(stats[:, 46:47], stats[:, 45:46]),
                waits=[W(sV)])
            sdrain()
            vop(lambda E: E.reciprocal(stats[:, 47:48], stats[:, 46:47]),
                waits=[W(sS)])
            vdrain()
            vop(lambda E: E.scalar_tensor_tensor(
                stats[:, 48:49], stats[:, 42:43], -1.0, stats[:, 47:48],
                OP.mult, OP.mult), inc=True)
            vdrain()

        def bcast_cols(cols, nsub, dst_ps, waits):
            for s4 in range(nsub):
                op("tensor",
                   lambda E, s4=s4: E.transpose(
                       P[5][0:1, 128 * s4:128 * s4 + 128],
                       stats[:, cols + s4:cols + s4 + 1], idf[:]),
                   waits=(waits if s4 == 0 else ()),
                   incs=([(sT, 1)] if s4 == nsub - 1 else ()))
            vop(lambda E, nsub=nsub: E.tensor_copy(
                rowt[0:1, 0:nsub * 128], P[5][0:1, 0:nsub * 128]),
                waits=[W(sT)], inc=True)
            vdrain()
            op("tensor",
               lambda E, dst_ps=dst_ps, nsub=nsub: E.matmul(
                   dst_ps[:, 0:nsub * 128], onesr[0:1, :],
                   rowt[0:1, 0:nsub * 128], start=True, stop=True),
               waits=[W(sV)], incs=[(sT, 1)])

        INTF = 0

        def emit_ffn(layer):
            dst = d_rs if layer == 1 else d_out
            src = d_x if layer == 1 else d_rs
            for q in range(NQ):
                for s in range(NSUB):
                    S = q * NSUB + s
                    xsl = xt[:, s * D:(s + 1) * D]
                    dma("sync", xsl, src[S * 128:(S + 1) * 128, :],
                        waits=[W(sT), W(sV), W(sS)])
                    ld_waits = [W(sD)]
                    ln_stats(xsl, ld_waits)
                    sop(lambda E, xsl=xsl: E.activation(
                        xn[:], xsl, AF.Identity, bias=stats[:, 48:49],
                        scale=stats[:, 47:48]), waits=[W(sV)])
                    sdrain()
                    vop(lambda E: E.reduce_max(
                        stats[:, 49:50], xn[:], axis=AXX,
                        apply_absolute_value=True), waits=[W(sS)])
                    vdrain()
                    vop(lambda E: E.tensor_scalar(
                        stats[:, 50:51], stats[:, 49:50], EPS, None, OP.max))
                    vdrain()
                    vop(lambda E: E.reciprocal(stats[:, 51:52],
                                               stats[:, 50:51]))
                    vdrain()
                    vop(lambda E: E.tensor_scalar(
                        stats[:, 52:53], stats[:, 51:52], 128.0, None,
                        OP.mult))
                    vop(lambda E, s=s: E.tensor_scalar(
                        stats[:, 20 + s:21 + s], stats[:, 50:51], INV128,
                        None, OP.mult), inc=True)
                    vdrain()
                    sop(lambda E: E.activation(
                        tq1[:], xn[:], AF.Identity, scale=stats[:, 52:53]),
                        waits=[W(sV)])
                    sdrain()
                    vop(lambda E: E.tensor_scalar(
                        tq2[:], tq1[:], M1, M1, OP.add, OP.subtract),
                        waits=[W(sS)])
                    vdrain()
                    vop(lambda E: E.tensor_scalar(
                        xn[:], tq2[:], M2, M2, OP.add, OP.subtract))
                    vdrain()
                    vop(lambda E: E.scalar_tensor_tensor(
                        tq1[:], tq2[:], G1M_AP, xn[:], OP.mult, OP.add))
                    vdrain()
                    vop(lambda E: E.tensor_scalar(
                        xqb[:], tq1[:], -128.0, 127.0, OP.max, OP.min),
                        inc=True)
                    vdrain()
                    vv = sV.v
                    for hc in range(16):
                        pb = PBa if hc < 8 else PBb
                        off = (hc % 8) * 128
                        op("tensor",
                           lambda E, pb=pb, off=off, hc=hc: E.transpose(
                               pb[:, off:off + 128],
                               xqb[:, hc * 128:(hc + 1) * 128], idb[:]),
                           waits=([W(sV, vv)] if hc == 0 else ()),
                           incs=([(sT, 1)] if hc in (7, 15) else ()))
                    tv = sT.v
                    vop(lambda E, s=s: E.tensor_copy(
                        xqT[:, s * 2048:s * 2048 + 1024], PBa[:]),
                        waits=[W(sT, tv - 1), W(sG)])
                    vop(lambda E, s=s: E.tensor_copy(
                        xqT[:, s * 2048 + 1024:s * 2048 + 2048], PBb[:]),
                        waits=[W(sT, tv)], inc=True)
                    vdrain()
                bcast_cols(20, NSUB, P[4], waits=[W(sV)])
                vop(lambda E: E.tensor_scalar(
                    sbt[:, 0:QT], P[4][:, 0:QT], SWG_AP, None, OP.mult),
                    waits=[W(sT)])
                vdrain()
                vop(lambda E: E.tensor_scalar(
                    sbt2[:, 0:QT], P[4][:, 0:QT], SWU_AP, None, OP.mult),
                    inc=True)
                vdrain()
                xqT_rhs = xqT[:].rearrange("p (s c) -> p s c", c=2048)
                mm_done = {}
                for p in range(64):
                    db = p % 2
                    gsl = wsl[:, db * 4096:db * 4096 + 2048]
                    usl = wsl[:, db * 4096 + 2048:db * 4096 + 4096]
                    dsem = sDab[p % 2]
                    dw = ([W(sT, mm_done[p - 2])] if p >= 2 else []) + \
                        [W(dsem)]
                    dma("sync",
                        gsl.rearrange("p (hc i) -> p hc i", i=128),
                        d_wgt[p].rearrange("(hc hp) i -> hp hc i", hp=128),
                        waits=dw, sem=dsem)
                    dma("sync",
                        usl.rearrange("p (hc i) -> p hc i", i=128),
                        d_wut[p].rearrange("(hc hp) i -> hp hc i", hp=128),
                        sem=dsem)
                    ldv = dsem.v
                    pg = P[p % 2]
                    pu = P[2 + p % 2]
                    for hc in range(16):
                        rhs = xqT_rhs[:, :, hc * 128:(hc + 1) * 128]
                        op("tensor",
                           lambda E, pg=pg, gsl=gsl, hc=hc, rhs=rhs:
                           E.matmul(pg[:, 0:QT],
                                    gsl[:, hc * 128:(hc + 1) * 128],
                                    rhs, start=(hc == 0), stop=(hc == 15)),
                           waits=([W(dsem, ldv), W(sV)] if hc == 0 else ()))
                        op("tensor",
                           lambda E, pu=pu, usl=usl, hc=hc, rhs=rhs:
                           E.matmul(pu[:, 0:QT],
                                    usl[:, hc * 128:(hc + 1) * 128],
                                    rhs, start=(hc == 0), stop=(hc == 15)),
                           incs=([(sT, 1)] if hc == 15 else ()))
                    mm_done[p] = sT.v
                    vop(lambda E, pg=pg: E.tensor_tensor(
                        ga[:, 0:QT], pg[:, 0:QT], sbt[:, 0:QT], OP.mult),
                        waits=[W(sT, mm_done[p])], inc=True)
                    vdrain()
                    sop(lambda E: E.activation(gb[:, 0:QT], ga[:, 0:QT],
                                               AF.Silu), waits=[W(sV)])
                    sdrain()
                    vop(lambda E, pu=pu: E.tensor_tensor(
                        gc[:, 0:QT], pu[:, 0:QT], sbt2[:, 0:QT], OP.mult))
                    vdrain()
                    vop(lambda E, p=p: E.tensor_tensor(
                        arf(INTF + p * QT, QT), gb[:, 0:QT], gc[:, 0:QT],
                        OP.mult), waits=[W(sS)], inc=True)
                    vdrain()
                ev_done = {}
                for p in range(64):
                    ph = P[4 + p % 2]
                    if p == 0:
                        w0 = [W(sV)]
                    elif p >= 2:
                        w0 = [W(sS, ev_done[p - 2])]
                    else:
                        w0 = []
                    op("tensor",
                       lambda E, ph=ph, p=p: E.matmul(
                           ph[:, 0:QT], h128[:], arf(INTF + p * QT, QT),
                           start=True, stop=True),
                       waits=w0, incs=[(sT, 1)])
                    tv = sT.v
                    sop(lambda E, ph=ph, p=p: E.copy(
                        arf(INTF + p * QT, QT), ph[:, 0:QT]),
                        waits=[W(sT, tv)])
                    ev_done[p] = sS.v
                first = True
                for hp in (1, 2, 4, 8, 16, 32):
                    view = AR[:, INTF:INTF + 64 * QT].rearrange(
                        "p (a two b) -> p a two b", two=2, b=hp * QT)
                    ev = view[:, :, 0, :]
                    od = view[:, :, 1, :]
                    vop(lambda E, ev=ev, od=od: E.tensor_tensor(
                        ev, ev, od, OP.add),
                        waits=([W(sS)] if first else ()))
                    first = False
                    vdrain()
                    vop(lambda E, ev=ev, od=od: E.scalar_tensor_tensor(
                        od, od, -2.0, ev, OP.mult, OP.add))
                    vdrain()
                vop(lambda E: E.tensor_copy(ga[:, 0:QT], arf(INTF, QT)))
                vop(lambda E: E.tensor_copy(gc[:, 0:QT], arf(INTF, QT)))
                vdrain()
                cur = 0
                for p in range(1, 64):
                    a, b = (ga, gb) if cur == 0 else (gb, ga)
                    c_, d_ = (gc, gd) if cur == 0 else (gd, gc)
                    vop(lambda E, a=a, b=b, p=p: E.tensor_tensor(
                        b[:, 0:QT], a[:, 0:QT], arf(INTF + p * QT, QT),
                        OP.max))
                    vop(lambda E, c_=c_, d_=d_, p=p: E.tensor_tensor(
                        d_[:, 0:QT], c_[:, 0:QT], arf(INTF + p * QT, QT),
                        OP.min))
                    vdrain()
                    cur ^= 1
                a = ga if cur == 0 else gb
                c_ = gc if cur == 0 else gd
                vop(lambda E, c_=c_: E.tensor_scalar(
                    sbt[:, 0:QT], c_[:, 0:QT], -1.0, None, OP.mult))
                vdrain()
                vop(lambda E, a=a: E.tensor_tensor(
                    sbt2[:, 0:QT], a[:, 0:QT], sbt[:, 0:QT], OP.max),
                    inc=True)
                vdrain()
                vv = sV.v
                for s4 in range(NSUB):
                    op("tensor",
                       lambda E, s4=s4: E.transpose(
                           P[4][:, s4 * 128:(s4 + 1) * 128],
                           sbt2[:, s4 * 128:(s4 + 1) * 128], idf[:]),
                       waits=([W(sV, vv)] if s4 == 0 else ()),
                       incs=([(sT, 1)] if s4 == NSUB - 1 else ()))
                for s4 in range(NSUB):
                    vop(lambda E, s4=s4: E.reduce_max(
                        stats[:, 24 + s4:25 + s4],
                        P[4][:, s4 * 128:(s4 + 1) * 128], axis=AXX),
                        waits=([W(sT)] if s4 == 0 else ()))
                vdrain()
                for s4 in range(NSUB):
                    vop(lambda E, s4=s4: E.tensor_scalar(
                        stats[:, 28 + s4:29 + s4], stats[:, 24 + s4:25 + s4],
                        HNORM, EPS, OP.mult, OP.max))
                vdrain()
                for s4 in range(NSUB):
                    vop(lambda E, s4=s4: E.reciprocal(
                        stats[:, 24 + s4:25 + s4],
                        stats[:, 28 + s4:29 + s4]))
                vdrain()
                for s4 in range(NSUB):
                    vop(lambda E, s4=s4: E.tensor_scalar(
                        stats[:, 32 + s4:33 + s4], stats[:, 24 + s4:25 + s4],
                        128.0, None, OP.mult))
                vdrain()
                for s4 in range(NSUB):
                    vop(lambda E, s4=s4: E.tensor_scalar(
                        stats[:, 36 + s4:37 + s4], stats[:, 32 + s4:33 + s4],
                        HNORM, None, OP.mult))
                    vop(lambda E, s4=s4: E.tensor_scalar(
                        stats[:, 56 + s4:57 + s4], stats[:, 28 + s4:29 + s4],
                        INV128, None, OP.mult), inc=(s4 == NSUB - 1))
                vdrain()
                bcast_cols(36, NSUB, P[4], waits=[W(sV)])
                vop(lambda E: E.tensor_copy(sbt[:, 0:QT], P[4][:, 0:QT]),
                    waits=[W(sT)], inc=True)
                vdrain()
                for p in range(64):
                    ch = arf(INTF + p * QT, QT)
                    aqv = arb(INTF + p * QT, QT)
                    vop(lambda E, ch=ch: E.tensor_tensor(
                        ga[:, 0:QT], ch, sbt[:, 0:QT], OP.mult))
                    vdrain()
                    vop(lambda E: E.tensor_scalar(
                        gb[:, 0:QT], ga[:, 0:QT], M1, M1, OP.add,
                        OP.subtract))
                    vdrain()
                    vop(lambda E: E.tensor_scalar(
                        gc[:, 0:QT], gb[:, 0:QT], M2, M2, OP.add,
                        OP.subtract))
                    vdrain()
                    vop(lambda E: E.scalar_tensor_tensor(
                        gd[:, 0:QT], gb[:, 0:QT], G1M_AP, gc[:, 0:QT],
                        OP.mult, OP.add))
                    vdrain()
                    vop(lambda E, aqv=aqv: E.tensor_scalar(
                        aqv, gd[:, 0:QT], -128.0, 127.0, OP.max, OP.min),
                        inc=True)
                    vdrain()
                for s4 in range(NSUB):
                    vop(lambda E, s4=s4: E.tensor_scalar(
                        stats[:, 60 + s4:61 + s4], stats[:, 56 + s4:57 + s4],
                        SWD_AP, 0.5, OP.mult, OP.mult), inc=(s4 == NSUB - 1))
                vdrain()
                dslab_done = {}
                for o in range(4):
                    for p in range(64):
                        db = p % 2
                        dsl = wdsl[:, db * 512:(db + 1) * 512]
                        key = o * 64 + p
                        dsem = sDcd[p % 2]
                        dw = ([W(sT, dslab_done[key - 2])]
                              if key >= 2 else []) + [W(dsem)]
                        dma("sync", dsl, d_wdt[p, :, o * 512:(o + 1) * 512],
                            waits=dw, sem=dsem)
                        ldv = dsem.v
                        for s4 in range(NSUB):
                            aqt = arb(INTF + p * QT, QT)
                            w0 = ()
                            if s4 == 0:
                                w0 = [W(dsem, ldv)]
                                if p == 0:
                                    w0 = w0 + [W(sV), W(sS)]
                            op("tensor",
                               lambda E, s4=s4, dsl=dsl, aqt=aqt, p=p:
                               E.matmul(P[s4][:],
                                        aqt[:, s4 * 128:(s4 + 1) * 128],
                                        dsl[:], start=(p == 0),
                                        stop=(p == 63)),
                               waits=w0,
                               incs=([(sT, 1)] if s4 == NSUB - 1 else ()))
                        dslab_done[key] = sT.v
                    tv = sT.v
                    for s4 in range(NSUB):
                        S = q * NSUB + s4
                        oa = ga if s4 % 2 == 0 else gb
                        ob = gc if s4 % 2 == 0 else gd
                        sop(lambda E, s4=s4, oa=oa: E.activation(
                            oa[:], P[s4][:], AF.Identity,
                            scale=stats[:, 60 + s4:61 + s4]),
                            waits=[W(sT, tv), W(sV)])
                        sdrain()
                        sv = sS.v
                        vop(lambda E, s4=s4, o=o, oa=oa, ob=ob:
                            E.tensor_tensor(
                                ob[:],
                                xt[:, s4 * D + o * 512:s4 * D + o * 512
                                   + 512], oa[:], OP.add),
                            waits=[W(sS, sv), W(sG)], inc=True)
                        vdrain()
                        dma("gpsimd",
                            dst[S * 128:(S + 1) * 128,
                                o * 512:(o + 1) * 512],
                            ob[:], waits=[W(sV)])

        emit_ffn(1)

        X1H = 0
        X1L = 8192
        WSF = 16384
        WHI = 18432
        WLO = 19456

        for r in range(16):
            dma("sync", tq2[:, 0:D], d_wout[r * 128:(r + 1) * 128, :],
                waits=[W(sV), W(sG)])
            ldv = sD.v
            vop(lambda E: E.tensor_copy(xqb[:, 0:D], tq2[:, 0:D]),
                waits=[W(sD, ldv), W(sG)])
            vdrain()
            vop(lambda E: E.tensor_tensor(
                tq1[:].bitcast(BF16)[:, 0:D], tq2[:, 0:D], xqb[:, 0:D],
                OP.subtract), inc=True)
            vdrain()
            dma("gpsimd", d_wo2[0, r * 128:(r + 1) * 128, :], xqb[:, 0:D],
                waits=[W(sV)])
            dma("gpsimd", d_wo2[1, r * 128:(r + 1) * 128, :],
                tq1[:].bitcast(BF16)[:, 0:D])

        for S in range(8):
            dma("sync", xt[:, 0:D], d_rs[S * 128:(S + 1) * 128, :],
                waits=[W(sV), W(sS), W(sT), W(sG)])
            ln_stats(xt[:, 0:D], [W(sD)])
            sop(lambda E: E.activation(
                xn[:], xt[:, 0:D], AF.Identity, bias=stats[:, 48:49],
                scale=stats[:, 47:48]), waits=[W(sV)])
            sdrain()
            vop(lambda E: E.tensor_copy(xqb[:], xn[:]), waits=[W(sS)])
            vdrain()
            lobuf = tq1[:].bitcast(BF16)[:, 0:D]
            vop(lambda E, lobuf=lobuf: E.tensor_tensor(
                lobuf, xn[:], xqb[:], OP.subtract), inc=True)
            vdrain()
            vv = sV.v
            for hl in range(2):
                srcb = xqb[:] if hl == 0 else lobuf
                base = X1H if hl == 0 else X1L
                for hc in range(16):
                    pb = PBa if hc < 8 else PBb
                    off = (hc % 8) * 128
                    op("tensor",
                       lambda E, pb=pb, off=off, hc=hc, srcb=srcb:
                       E.transpose(pb[:, off:off + 128],
                                   srcb[:, hc * 128:(hc + 1) * 128],
                                   idb[:]),
                       waits=([W(sV, vv)] if (hl == 0 and hc == 0)
                              else ([W(sV)] if hc == 0 else ())),
                       incs=([(sT, 1)] if hc in (7, 15) else ()))
                tv = sT.v
                for g, pb in ((0, PBa), (1, PBb)):
                    for k in range(8):
                        mc = g * 8 + k
                        vop(lambda E, pb=pb, k=k, mc=mc, base=base, S=S:
                            E.tensor_copy(
                                arb(base, 16384)[
                                    :, mc * 1024 + S * 128:
                                    mc * 1024 + S * 128 + 128],
                                pb[:, k * 128:(k + 1) * 128]),
                            waits=([W(sT, tv - 1 + g)] if k == 0 else ()),
                            inc=(k == 7 and g == 1))
                vdrain()

        wq_done = {}
        for oc in range(48):
            wsf = arf(WSF, 2048)
            dw = [W(sV)] if oc == 0 else ([W(sV, wq_done[oc - 1])]
                                          if oc >= 1 else [])
            dma("sync",
                wsf.rearrange("p (mc o) -> p mc o", o=128),
                d_wqkv[:, oc * 128:(oc + 1) * 128].rearrange(
                    "(mc mp) o -> mp mc o", mp=128),
                waits=dw)
            ldv = sD.v
            vop(lambda E, wsf=wsf: E.tensor_copy(arb(WHI, 2048), wsf),
                waits=[W(sD, ldv)])
            vdrain()
            vop(lambda E, wsf=wsf: E.tensor_tensor(
                arb(WLO, 2048), wsf, arb(WHI, 2048), OP.subtract),
                inc=True)
            vdrain()
            wq_done[oc] = sV.v
            vvw = sV.v
            pq = [P[oc % 2], P[2 + oc % 2]]
            for mc in range(16):
                whi_t = arb(WHI, 2048)[:, mc * 128:(mc + 1) * 128]
                wlo_t = arb(WLO, 2048)[:, mc * 128:(mc + 1) * 128]
                xhi = arb(X1H, 16384)[:, mc * 1024:(mc + 1) * 1024]
                xlo = arb(X1L, 16384)[:, mc * 1024:(mc + 1) * 1024]
                pairs = [(whi_t, xhi), (whi_t, xlo), (wlo_t, xhi)]
                for pi, (lhs, rhs) in enumerate(pairs):
                    st = (mc == 0 and pi == 0)
                    sp = (mc == 15 and pi == 2)
                    for half in range(2):
                        op("tensor",
                           lambda E, lhs=lhs, rhs=rhs, st=st, sp=sp,
                           half=half, pq=pq: E.matmul(
                               pq[half][:],
                               lhs, rhs[:, half * 512:(half + 1) * 512],
                               start=st, stop=sp),
                           waits=([W(sV, vvw)]
                                  if (st and half == 0) else ()),
                           incs=([(sT, 1)] if (sp and half == 1) else ()))
            tv = sT.v
            sop(lambda E, pq=pq, oc=oc: E.activation(
                tq2[:, 0:512], pq[0][:], AF.Identity,
                bias=bqf[:, oc:oc + 1]), waits=[W(sT, tv), W(sV)])
            sop(lambda E, pq=pq, oc=oc: E.activation(
                tq2[:, 512:1024], pq[1][:], AF.Identity,
                bias=bqf[:, oc:oc + 1]))
            sdrain()
            sv = sS.v
            for s8 in range(8):
                op("tensor",
                   lambda E, s8=s8: E.transpose(
                       P[4 + s8 % 2][:, (s8 // 2 % 2) * 128:
                                     (s8 // 2 % 2) * 128 + 128],
                       tq2[:, s8 * 128:(s8 + 1) * 128], idf[:]),
                   waits=([W(sS, sv)] if s8 == 0 else
                          ([W(sV)] if s8 >= 2 else ())),
                   incs=[(sT, 1)])
                tv2 = sT.v
                vop(lambda E, s8=s8: E.tensor_copy(
                    gc[:, 0:128] if s8 % 2 == 0 else gd[:, 0:128],
                    P[4 + s8 % 2][:, (s8 // 2 % 2) * 128:
                                  (s8 // 2 % 2) * 128 + 128]),
                    waits=[W(sT, tv2), W(sG)], inc=True)
                vdrain()
                dma("gpsimd",
                    d_qs[s8 * 128:(s8 + 1) * 128,
                         oc * 128:(oc + 1) * 128],
                    gc[:, 0:128] if s8 % 2 == 0 else gd[:, 0:128],
                    waits=[W(sV)])

        QV = 0
        SC = 6144
        PE = 6400
        DN = 6656
        ATN = 6720
        ATT = 8768
        ATH = 10816
        ATL = 11840
        WOS = 12864
        NPAIR = 256

        wo_done = {}
        for S in range(8):
            dma("sync", arf(QV, 6144), d_qs[S * 128:(S + 1) * 128, :],
                waits=[W(sT), W(sV), W(sS)])
            ldv = sD.v
            qall = arf(QV, 6144).rearrange("p (h g d) -> p h g d",
                                           g=3, d=128)
            mtv = arf(ATT, 2048).rearrange("p (h d) -> p h d", d=128)
            scv = arf(SC, 256).rearrange("p (h1 h2) -> p h1 h2", h2=16)
            vop(lambda E: E.memset(arf(SC, 256), 0.0),
                waits=[W(sD, ldv), W(sG)])
            vdrain()
            for h2 in range(16):
                nh = 16 - h2
                kb = arf(QV + h2 * 384 + 128, 128).unsqueeze(1) \
                    .broadcast_to([128, nh, 128])
                vop(lambda E, h2=h2, nh=nh, kb=kb: E.tensor_tensor(
                    mtv[:, h2:16, :], qall[:, h2:16, 0, :], kb, OP.mult))
                vdrain()
                vop(lambda E, h2=h2, nh=nh: E.tensor_reduce(
                    scv[:, h2:16, h2:h2 + 1], mtv[:, h2:16, :],
                    axis=AXX, op=OP.add))
                vdrain()
            op("vector", lambda E: E.drain(), incs=[(sV, 1)])
            sop(lambda E: E.activation(
                arf(PE, 256)[:, 0:NPAIR], arf(SC, 256)[:, 0:NPAIR],
                AF.Exp, bias=NEGC_AP, scale=ISQ), waits=[W(sV)])
            sdrain()
            for h1 in range(16):
                vop(lambda E, h1=h1: E.reduce_sum(
                    arf(DN, 32)[:, h1:h1 + 1],
                    arf(PE, 256)[:, h1 * 16:h1 * 16 + h1 + 1], axis=AXX),
                    waits=([W(sS)] if h1 == 0 else ()))
            vdrain()
            vop(lambda E: E.reciprocal(arf(DN, 32)[:, 16:32],
                                       arf(DN, 32)[:, 0:16]))
            vdrain()
            vop(lambda E: E.tensor_tensor(
                arf(DN, 64)[:, 32:48], arf(DN, 32)[:, 0:16],
                arf(DN, 32)[:, 16:32], OP.mult), inc=True)
            vdrain()
            vop(lambda E: E.tensor_copy(ga[:, 0:128],
                                        arf(QV + 256, 128)))
            vdrain()
            curv = 0
            for h2 in range(1, 16):
                a_, b_ = (ga, gb) if curv == 0 else (gb, ga)
                vop(lambda E, a_=a_, b_=b_, h2=h2: E.tensor_tensor(
                    b_[:, 0:128], a_[:, 0:128],
                    arf(QV + h2 * 384 + 256, 128), OP.add))
                vdrain()
                curv ^= 1
            vsb = ga if curv == 0 else gb
            for h1 in range(16):
                vop(lambda E, h1=h1, vsb=vsb: E.tensor_scalar(
                    arf(ATN + h1 * 128, 128), vsb[:, 0:128],
                    arf(DN, 64)[:, 32 + h1:33 + h1], None, OP.mult),
                    inc=(h1 == 15))
            vdrain()
            vv = sV.v
            for hc in range(16):
                op("tensor",
                   lambda E, hc=hc: E.transpose(
                       P[4 + (hc // 4) % 2][:, (hc % 4) * 128:
                                            (hc % 4) * 128 + 128],
                       arf(ATN + hc * 128, 128), idf[:]),
                   waits=([W(sV, vv)] if hc == 0 else
                          ([W(sV)] if hc % 4 == 0 and hc >= 8 else ())),
                   incs=([(sT, 1)] if hc % 4 == 3 else ()))
                if hc % 4 == 3:
                    tv2 = sT.v
                    g0 = hc - 3
                    vop(lambda E, g0=g0, hc=hc: E.tensor_copy(
                        arf(ATT + g0 * 128, 512),
                        P[4 + (hc // 4) % 2][:, 0:512]),
                        waits=[W(sT, tv2)], inc=True)
                    vdrain()
            vop(lambda E: E.tensor_copy(arb(ATH, 2048), arf(ATT, 2048)))
            vdrain()
            vop(lambda E: E.tensor_tensor(
                arb(ATL, 2048), arf(ATT, 2048), arb(ATH, 2048),
                OP.subtract), inc=True)
            vdrain()
            vva = sV.v
            for hd in range(16):
                db = hd % 2
                wsh = arb(WOS + db * 2048, 2048)
                wsl2 = arb(WOS + db * 2048 + 1024, 2048)
                key = S * 16 + hd
                dsem = sDef[hd % 2]
                dwx = ([W(sT, wo_done[key - 2])] if key >= 2 else []) + \
                    [W(dsem)]
                dma("sync", wsh, d_wo2[0, hd * 128:(hd + 1) * 128, :],
                    waits=dwx, sem=dsem)
                dma("sync", wsl2, d_wo2[1, hd * 128:(hd + 1) * 128, :],
                    sem=dsem)
                ldw = dsem.v
                ahi = arb(ATH, 2048)[:, hd * 128:(hd + 1) * 128]
                alo = arb(ATL, 2048)[:, hd * 128:(hd + 1) * 128]
                pairs = [(ahi, wsh), (ahi, wsl2), (alo, wsh)]
                for pi, (lhs, rhs) in enumerate(pairs):
                    st = (hd == 0 and pi == 0)
                    sp = (hd == 15 and pi == 2)
                    first = (pi == 0)
                    for quar in range(4):
                        op("tensor",
                           lambda E, lhs=lhs, rhs=rhs, st=st, sp=sp,
                           quar=quar: E.matmul(
                               P[quar][:], lhs,
                               rhs[:, quar * 512:(quar + 1) * 512],
                               start=st, stop=sp),
                           waits=([W(dsem, ldw), W(sV, vva)]
                                  if (first and quar == 0) else ()),
                           incs=([(sT, 1)]
                                 if (pi == 2 and quar == 3) else ()))
                wo_done[key] = sT.v
            tv = sT.v
            dma("sync", xt[:, 0:D], d_rs[S * 128:(S + 1) * 128, :])
            ldx = sD.v
            for quar in range(4):
                sop(lambda E, quar=quar: E.activation(
                    ga[:] if quar % 2 == 0 else gb[:], P[quar][:],
                    AF.Identity),
                    waits=(([W(sT, tv)] if quar == 0 else []) +
                           [W(sV), W(sG)]))
                sdrain()
                sv = sS.v
                vop(lambda E, quar=quar: E.tensor_tensor(
                    gc[:] if quar % 2 == 0 else gd[:],
                    ga[:] if quar % 2 == 0 else gb[:],
                    boutr[:, quar * 512:(quar + 1) * 512], OP.add),
                    waits=[W(sS, sv), W(sD, ldx), W(sG)])
                vdrain()
                vop(lambda E, quar=quar: E.tensor_tensor(
                    ga[:] if quar % 2 == 1 else gb[:],
                    gc[:] if quar % 2 == 0 else gd[:],
                    xt[:, quar * 512:(quar + 1) * 512], OP.add),
                    inc=True)
                vdrain()
                dma("gpsimd",
                    d_rs[S * 128:(S + 1) * 128,
                         quar * 512:(quar + 1) * 512],
                    ga[:] if quar % 2 == 1 else gb[:], waits=[W(sV)])

        emit_ffn(3)

        if debug:
            d_dbg = {}
            d_dbg["dbg_rs"] = nc.dram_tensor("dbg_rs", [TPC, D], F32,
                                             kind="ExternalOutput").ap()
            d_dbg["dbg_qs"] = nc.dram_tensor("dbg_qs", [TPC, 6144], F32,
                                             kind="ExternalOutput").ap()
            d_dbg["dbg_co"] = nc.dram_tensor("dbg_co", [128, 16], F32,
                                             kind="ExternalOutput").ap()
            d_dbg["dbg_wgt"] = nc.dram_tensor("dbg_wgt", [64, D, 128], BF16,
                                              kind="ExternalOutput").ap()
            dma("gpsimd", d_dbg["dbg_rs"], d_rs, waits=[W(sG), W(sD)])
            dma("gpsimd", d_dbg["dbg_qs"], d_qs)
            dma("gpsimd", d_dbg["dbg_co"], consts[:])
            dma("gpsimd", d_dbg["dbg_wgt"], d_wgt)

        op("gpsimd", lambda E: None, waits=[W(sG)])
        op("sync", lambda E: None, waits=[W(sD)])

        blk = es.enter_context(nc.Block())
        sch.emit(blk)
    return nc


def _consts():
    idf = np.eye(128, dtype=np.float32)
    idb = np.eye(128, dtype=ml_dtypes.bfloat16)
    h = np.array([[1.0]], dtype=np.float32)
    while h.shape[0] < 128:
        h = np.block([[h, h], [h, -h]])
    h128 = h.astype(np.float32)
    return idf, idb, h128


def _inputs_for_core(c, x, wqkv, bqkv, wout, bout, wg, wu, wd, consts):
    idf, idb, h128 = consts
    wqkv_f = np.ascontiguousarray(wqkv.reshape(D, 6144))
    wout_f = np.ascontiguousarray(wout.reshape(D, D))
    bq_f = np.ascontiguousarray(
        bqkv.reshape(6144).reshape(48, 128).T.astype(np.float32))
    return {
        "x": np.ascontiguousarray(x[c * TPC:(c + 1) * TPC, :]),
        "wg": wg, "wu": wu, "wd": wd,
        "wqkv": wqkv_f, "wout": wout_f,
        "bq_f": bq_f,
        "bout_r": np.ascontiguousarray(
            np.tile(bout[None, :], (128, 1)).astype(np.float32)),
        "ident_f": idf, "ident_b": idb, "h128": h128,
    }


def kernel(L0_attn_x, L0_attn_Wqkv, L0_attn_bqkv, L0_attn_Wout, L0_attn_bout,
           L0_ffn_w_gate, L0_ffn_w_up, L0_ffn_w_down, _debug=False):
    x = np.ascontiguousarray(
        np.asarray(L0_attn_x, dtype=np.float32).reshape(8192, D))
    wqkv = np.ascontiguousarray(np.asarray(L0_attn_Wqkv, dtype=np.float32))
    bqkv = np.ascontiguousarray(np.asarray(L0_attn_bqkv, dtype=np.float32))
    wout = np.ascontiguousarray(np.asarray(L0_attn_Wout, dtype=np.float32))
    bout = np.ascontiguousarray(np.asarray(L0_attn_bout, dtype=np.float32))
    wg = np.ascontiguousarray(np.asarray(L0_ffn_w_gate, dtype=np.float32))
    wu = np.ascontiguousarray(np.asarray(L0_ffn_w_up, dtype=np.float32))
    wd = np.ascontiguousarray(np.asarray(L0_ffn_w_down, dtype=np.float32))
    consts = _consts()
    nc = build_program(debug=_debug)
    in_maps = [_inputs_for_core(c, x, wqkv, bqkv, wout, bout, wg, wu, wd,
                                consts) for c in range(8)]
    res = run_bass_kernel_spmd(nc, in_maps, core_ids=list(range(8)))
    out = np.concatenate([res.results[c]["out"] for c in range(8)], axis=0)
    if _debug:
        return out.reshape(1, 8192, D).astype(np.float32), res.results
    return out.reshape(1, 8192, D).astype(np.float32)


# version 03 (final)
# speedup vs baseline: 1.2290x; 1.2290x over previous
import math
import contextlib
import numpy as np
import ml_dtypes
import concourse.bass as bass
import concourse.mybir as mybir
from concourse.bass_utils import run_bass_kernel_spmd

F32 = mybir.dt.float32
BF16 = mybir.dt.bfloat16
AF = mybir.ActivationFunctionType
OP = mybir.AluOpType
AXX = mybir.AxisListType.X

M1 = 12582912.0
M2 = 6755399441055744.0
HNORM = float(np.float32(1.0 / np.sqrt(8192.0)))
ISQ = 1.0 / math.sqrt(128.0)
NEGC = -10.0
EPS = 1e-5
INV2048 = 1.0 / 2048.0
INV128 = 1.0 / 128.0

TPC = 1024
D = 2048
I8K = 8192
QT = 256
NQ = TPC // QT
NSUB = QT // 128

ENGINES = ("sync", "tensor", "vector", "scalar", "gpsimd")


class Sem:
    def __init__(self, handle):
        self.h = handle
        self.v = 0


class Sched:
    def __init__(self):
        self.ops = {e: [] for e in ENGINES}

    def add(self, eng, fn, waits=(), incs=()):
        wl = [(s.h, int(v)) for (s, v) in waits]
        il = []
        for (s, amt) in incs:
            s.v += amt
            il.append((s.h, amt))
        self.ops[eng].append((wl, fn, il))

    def emit(self, block):
        def runner(eng):
            def run(E):
                for (wl, fn, il) in self.ops[eng]:
                    for (h, v) in wl:
                        E.wait_ge(h, v)
                    ins = fn(E)
                    for (h, amt) in il:
                        ins = ins.then_inc(h, amt)
            return run
        block.sync(runner("sync"))
        block.tensor(runner("tensor"))
        block.vector(runner("vector"))
        block.scalar(runner("scalar"))
        block.gpsimd(runner("gpsimd"))


def build_program(debug=False):
    nc = bass.Bass("TRN2", target_bir_lowering=False)

    d_x = nc.dram_tensor("x", [TPC, D], F32, kind="ExternalInput").ap()
    d_wg = nc.dram_tensor("wg", [I8K, D], F32, kind="ExternalInput").ap()
    d_wu = nc.dram_tensor("wu", [I8K, D], F32, kind="ExternalInput").ap()
    d_wd = nc.dram_tensor("wd", [D, I8K], F32, kind="ExternalInput").ap()
    d_wqkv = nc.dram_tensor("wqkv", [D, 6144], F32, kind="ExternalInput").ap()
    d_wout = nc.dram_tensor("wout", [D, D], F32, kind="ExternalInput").ap()
    d_bqf = nc.dram_tensor("bq_f", [128, 48], F32, kind="ExternalInput").ap()
    d_bout = nc.dram_tensor("bout_r", [128, D], F32, kind="ExternalInput").ap()
    d_idf = nc.dram_tensor("ident_f", [128, 128], F32, kind="ExternalInput").ap()
    d_idb = nc.dram_tensor("ident_b", [128, 128], BF16, kind="ExternalInput").ap()
    d_h128 = nc.dram_tensor("h128", [128, 128], F32, kind="ExternalInput").ap()
    d_out = nc.dram_tensor("out", [TPC, D], F32, kind="ExternalOutput").ap()

    d_rs = nc.dram_tensor("rs", [TPC, D], F32).ap()
    d_wgt = nc.dram_tensor("wgt_s", [64, D, 128], BF16).ap()
    d_wut = nc.dram_tensor("wut_s", [64, D, 128], BF16).ap()
    d_wdt = nc.dram_tensor("wdt_s", [64, 128, D], BF16).ap()
    d_qs = nc.dram_tensor("qs_s", [TPC, 6144], F32).ap()
    d_wo2 = nc.dram_tensor("wo2_s", [2, D, D], BF16).ap()

    es = contextlib.ExitStack()
    with es:
        def sbuf(name, shape, dt):
            return es.enter_context(nc.sbuf_tensor(name, shape, dt))

        def psum(name, shape, dt):
            return es.enter_context(nc.psum_tensor(name, shape, dt))

        AR = sbuf("arena", [128, 20480], F32)
        xt = sbuf("xt", [128, 2 * D], F32)
        xn = sbuf("xn", [128, D], F32)
        tq1 = sbuf("tq1", [128, D], F32)
        tq2 = sbuf("tq2", [128, D], F32)
        xqb = sbuf("xqb", [128, D], BF16)
        xqT = sbuf("xqT", [128, 4096], BF16)
        wsl = sbuf("wslab", [128, 8192], BF16)
        wdsl = sbuf("wdslab", [128, 4096], BF16)
        ga = sbuf("ga", [128, 512], F32)
        gb = sbuf("gb", [128, 512], F32)
        gc = sbuf("gc", [128, 512], F32)
        gd = sbuf("gd", [128, 512], F32)
        sbt = sbuf("sbt", [128, 512], F32)
        sbt2 = sbuf("sbt2", [128, 512], F32)
        rowt = sbuf("rowt", [1, 512], F32)
        onesr = sbuf("onesr", [1, 128], F32)
        stats = sbuf("stats", [128, 64], F32)
        consts = sbuf("consts", [128, 16], F32)
        idf = sbuf("idf", [128, 128], F32)
        idb = sbuf("idb", [128, 128], BF16)
        h128 = sbuf("h128t", [128, 128], F32)
        bqf = sbuf("bqf", [128, 48], F32)
        boutr = sbuf("boutr", [128, D], F32)

        P = [psum("P%d" % i, [128, 512], F32) for i in range(6)]
        PBa = psum("PBa", [128, 1024], BF16)
        PBb = psum("PBb", [128, 1024], BF16)

        sD = Sem(es.enter_context(nc.semaphore("sD")))
        sG = Sem(es.enter_context(nc.semaphore("sG")))
        sT = Sem(es.enter_context(nc.semaphore("sT")))
        sV = Sem(es.enter_context(nc.semaphore("sV")))
        sS = Sem(es.enter_context(nc.semaphore("sS")))
        sC = Sem(es.enter_context(nc.semaphore("sC")))
        sDab = [Sem(es.enter_context(nc.semaphore("sDa"))),
                Sem(es.enter_context(nc.semaphore("sDb")))]
        sDcd = [Sem(es.enter_context(nc.semaphore("sDc"))),
                Sem(es.enter_context(nc.semaphore("sDd")))]
        sDef = [Sem(es.enter_context(nc.semaphore("sDe"))),
                Sem(es.enter_context(nc.semaphore("sDf")))]

        sch = Sched()

        def op(eng, fn, waits=(), incs=()):
            sch.add(eng, fn, waits=waits, incs=incs)

        def W(sem, v=None):
            return (sem, sem.v if v is None else v)

        def dma(eng, out, in_, waits=(), sem=None):
            if sem is None:
                sem = sD if eng == "sync" else sG
            op(eng, lambda E, o=out, i=in_: E.dma_start(out=o, in_=i),
               waits=waits, incs=[(sem, 16)])

        def vop(fn, waits=(), inc=False):
            op("vector", fn, waits=waits, incs=([(sV, 1)] if inc else ()))

        def sop(fn, waits=(), inc=True):
            op("scalar", fn, waits=waits, incs=([(sS, 1)] if inc else ()))

        def vdrain():
            op("vector", lambda E: E.drain())

        def sdrain():
            op("scalar", lambda E: E.drain())

        def arf(c0, n):
            return AR[:, c0:c0 + n]

        def arb(c0, nbf):
            return AR[:, c0:c0 + nbf // 2].bitcast(BF16)

        G1M_AP = consts[:, 1:2]
        NEGC_AP = consts[:, 2:3]
        SWG_AP = consts[:, 3:4]
        SWU_AP = consts[:, 4:5]
        SWD_AP = consts[:, 5:6]
        TH_AP = [consts[:, 6:7], consts[:, 7:8], consts[:, 8:9]]
        NTH_AP = [consts[:, 9:10], consts[:, 10:11], consts[:, 11:12]]

        dma("sync", idf[:], d_idf)
        dma("sync", idb[:], d_idb)
        dma("sync", h128[:], d_h128)
        dma("sync", bqf[:], d_bqf)
        dma("sync", boutr[:], d_bout)

        vop(lambda E: E.memset(consts[:, 13:16], 0.0))
        vop(lambda E: E.memset(consts[:, 2:3], NEGC))
        vop(lambda E: E.memset(consts[:, 12:13], 1.0))
        vop(lambda E: E.memset(onesr[:, :], 1.0))
        vdrain()
        vop(lambda E: E.tensor_scalar(consts[:, 0:1], consts[:, 12:13],
                                      M2, M2, OP.add, OP.subtract))
        vdrain()
        vop(lambda E: E.tensor_scalar(consts[:, 1:2], consts[:, 0:1],
                                      -1.0, 1.0, OP.mult, OP.add), inc=True)
        vdrain()

        for mi in range(3):
            vop(lambda E, m=mi: E.memset(stats[:, 32 + m:33 + m], 0.0),
                inc=(mi == 2))
        vdrain()

        def mean_chunks(mi):
            if mi == 0:
                return [d_wg[p * 128:(p + 1) * 128, :] for p in range(64)]
            if mi == 1:
                return [d_wu[p * 128:(p + 1) * 128, :] for p in range(64)]
            return [d_wd[rt * 128:(rt + 1) * 128, q2 * D:(q2 + 1) * D]
                    for rt in range(16) for q2 in range(4)]

        cnt = 0
        abs_done = {}
        for mi in range(3):
            chunks = mean_chunks(mi)
            for ci, ch in enumerate(chunks):
                par = cnt % 2
                xsl = xt[:, par * D:(par + 1) * D]
                dsem = sDab[par]
                dmaw = ([W(sS, abs_done[cnt - 2])] if cnt >= 2 else []) + \
                    [W(dsem)]
                dma("sync", xsl, ch, waits=dmaw, sem=dsem)
                ldv = dsem.v
                tqx = tq1 if par == 0 else tq2
                sop(lambda E, c=ci % 8, xsl=xsl, tqx=tqx: E.activation(
                    tqx[:], xsl, AF.Abs,
                    accum_out=stats[:, c:c + 1]),
                    waits=[W(dsem, ldv), W(sV)])
                abs_done[cnt] = sS.v
                cnt += 1
                if ci % 8 == 7:
                    vop(lambda E: E.reduce_sum(stats[:, 8:9],
                                               stats[:, 0:8], axis=AXX),
                        waits=[W(sS)])
                    vdrain()
                    vop(lambda E, m=mi: E.tensor_tensor(
                        stats[:, 9:10], stats[:, 32 + m:33 + m],
                        stats[:, 8:9], OP.add))
                    vdrain()
                    vop(lambda E, m=mi: E.tensor_copy(
                        stats[:, 32 + m:33 + m], stats[:, 9:10]),
                        inc=True)
                    vdrain()
        vop(lambda E: E.memset(stats[:, 35:36], 1.0), inc=True)
        vdrain()
        for mi in range(3):
            op("tensor",
               lambda E, m=mi: E.matmul(P[5][0:1, m:m + 1],
                                        stats[:, 32 + m:33 + m],
                                        stats[:, 35:36], start=True,
                                        stop=True),
               waits=([W(sV)] if mi == 0 else ()),
               incs=([(sT, 1)] if mi == 2 else ()))
        vop(lambda E: E.tensor_copy(rowt[0:1, 0:3], P[5][0:1, 0:3]),
            waits=[W(sT)], inc=True)
        vdrain()
        op("tensor",
           lambda E: E.matmul(P[5][:, 0:3], onesr[0:1, :], rowt[0:1, 0:3],
                              start=True, stop=True),
           waits=[W(sV)], incs=[(sT, 1)])
        for mi in range(3):
            vop(lambda E, m=mi: E.tensor_scalar(
                consts[:, 3 + m:4 + m], P[5][:, m:m + 1],
                1.0 / 16777216.0, EPS, OP.mult, OP.max),
                waits=([W(sT)] if mi == 0 else ()))
        vdrain()
        for mi in range(3):
            vop(lambda E, m=mi: E.tensor_scalar(
                consts[:, 6 + m:7 + m], consts[:, 3 + m:4 + m],
                0.5, None, OP.mult))
        vdrain()
        for mi in range(3):
            vop(lambda E, m=mi: E.tensor_scalar(
                consts[:, 9 + m:10 + m], consts[:, 6 + m:7 + m],
                -1.0, None, OP.mult), inc=(mi == 2))
        vdrain()

        def quant_wgwu(d_src, d_dst, mi, qdone):
            for p in range(64):
                par = p % 2
                xsl = xt[:, par * D:(par + 1) * D]
                tqx = tq1 if par == 0 else tq2
                dsem = sDab[par]
                dmaw = ([W(sV, qdone[p - 2])] if p >= 2 else [W(sV)]) + \
                    [W(dsem)]
                dma("sync", xsl, d_src[p * 128:(p + 1) * 128, :],
                    waits=dmaw, sem=dsem)
                ldv = dsem.v
                vop(lambda E, xsl=xsl, tqx=tqx, mi=mi: E.tensor_scalar(
                    tqx[:], xsl, NTH_AP[mi], None, OP.is_le),
                    waits=[W(dsem, ldv)])
                vdrain()
                vop(lambda E, mi=mi, xsl=xsl, tqx=tqx:
                    E.scalar_tensor_tensor(
                        xqb[:], xsl, TH_AP[mi], tqx[:], OP.is_ge,
                        OP.subtract),
                    waits=([W(sT, qdone[("t", p - 1)])] if p >= 1 else []),
                    inc=True)
                qdone[p] = sV.v
                vdrain()
                vv = sV.v
                for hc in range(16):
                    pb = PBa if hc < 8 else PBb
                    off = (hc % 8) * 128
                    op("tensor",
                       lambda E, pb=pb, off=off, hc=hc: E.transpose(
                           pb[:, off:off + 128],
                           xqb[:, hc * 128:(hc + 1) * 128], idb[:]),
                       waits=([W(sV, vv), W(sD)] if hc == 0 else ()),
                       incs=([(sT, 1)] if hc in (7, 15) else ()))
                tv = sT.v
                qdone[("t", p)] = tv
                vop(lambda E: E.tensor_copy(xqT[:, 0:1024], PBa[:]),
                    waits=[W(sT, tv - 1), W(sG)])
                vop(lambda E: E.tensor_copy(xqT[:, 1024:2048], PBb[:]),
                    waits=[W(sT, tv)], inc=True)
                vdrain()
                dma("gpsimd",
                    d_dst[p].rearrange("(hc hp) i -> hp hc i", hp=128),
                    xqT[:, 0:2048].rearrange("p (hc i) -> p hc i", i=128),
                    waits=[W(sV)])

        quant_wgwu(d_wg, d_wgt, 0, {})
        quant_wgwu(d_wu, d_wut, 1, {})

        wdq = arb(0, 8192)
        for rt in range(16):
            for half in range(2):
                for q2 in range(2):
                    c0 = half * 4096 + q2 * D
                    dma("sync", xt[:, q2 * D:(q2 + 1) * D],
                        d_wd[rt * 128:(rt + 1) * 128, c0:c0 + D],
                        waits=([W(sV)] if q2 == 0 else ()))
                ldv = sD.v
                for q2 in range(2):
                    vop(lambda E, q2=q2: E.tensor_scalar(
                        tq1[:], xt[:, q2 * D:(q2 + 1) * D], NTH_AP[2], None,
                        OP.is_le), waits=([W(sD, ldv)] if q2 == 0 else ()))
                    vdrain()
                    vop(lambda E, q2=q2, half=half: E.scalar_tensor_tensor(
                        wdq[:, half * 4096 + q2 * D:
                            half * 4096 + (q2 + 1) * D],
                        xt[:, q2 * D:(q2 + 1) * D], TH_AP[2], tq1[:],
                        OP.is_ge, OP.subtract), inc=(q2 == 1))
                    vdrain()
            vv = sV.v
            for grp in range(4):
                for k in range(16):
                    p = grp * 16 + k
                    pb = PBa if k < 8 else PBb
                    off = (k % 8) * 128
                    op("tensor",
                       lambda E, pb=pb, off=off, p=p: E.transpose(
                           pb[:, off:off + 128],
                           wdq[:, p * 128:(p + 1) * 128], idb[:]),
                       waits=([W(sV, vv)] if (grp == 0 and k == 0)
                              else ([W(sV)] if k == 0 else ())),
                       incs=([(sT, 1)] if k in (7, 15) else ()))
                tv = sT.v
                vop(lambda E: E.tensor_copy(xqT[:, 0:1024], PBa[:]),
                    waits=[W(sT, tv - 1), W(sG)])
                vop(lambda E: E.tensor_copy(xqT[:, 1024:2048], PBb[:]),
                    waits=[W(sT, tv)], inc=True)
                vdrain()
                for k in range(16):
                    p = grp * 16 + k
                    dma("gpsimd",
                        d_wdt[p, :, rt * 128:(rt + 1) * 128],
                        xqT[:, k * 128:(k + 1) * 128],
                        waits=([W(sV)] if k == 0 else ()))

        def ln_stats(xsl, ld_waits):
            vop(lambda E, xsl=xsl: E.reduce_sum(stats[:, 40:41], xsl,
                                                axis=AXX), waits=ld_waits)
            sop(lambda E, xsl=xsl: E.activation(
                tq1[:], xsl, AF.Square, accum_out=stats[:, 41:42]),
                waits=ld_waits)
            sdrain()
            vdrain()
            vop(lambda E: E.tensor_scalar(stats[:, 42:43], stats[:, 40:41],
                                          INV2048, None, OP.mult))
            vop(lambda E: E.tensor_scalar(stats[:, 43:44], stats[:, 41:42],
                                          INV2048, None, OP.mult),
                waits=[W(sS)])
            vdrain()
            vop(lambda E: E.tensor_tensor(stats[:, 44:45], stats[:, 42:43],
                                          stats[:, 42:43], OP.mult))
            vdrain()
            vop(lambda E: E.scalar_tensor_tensor(
                stats[:, 45:46], stats[:, 43:44], EPS, stats[:, 44:45],
                OP.add, OP.subtract), inc=True)
            vdrain()
            sop(lambda E: E.sqrt(stats[:, 46:47], stats[:, 45:46]),
                waits=[W(sV)])
            sdrain()
            vop(lambda E: E.reciprocal(stats[:, 47:48], stats[:, 46:47]),
                waits=[W(sS)])
            vdrain()
            vop(lambda E: E.scalar_tensor_tensor(
                stats[:, 48:49], stats[:, 42:43], -1.0, stats[:, 47:48],
                OP.mult, OP.mult), inc=True)
            vdrain()

        def bcast_cols(cols, nsub, dst_ps, waits):
            for s4 in range(nsub):
                op("tensor",
                   lambda E, s4=s4: E.transpose(
                       P[5][0:1, 128 * s4:128 * s4 + 128],
                       stats[:, cols + s4:cols + s4 + 1], idf[:]),
                   waits=(waits if s4 == 0 else ()),
                   incs=([(sT, 1)] if s4 == nsub - 1 else ()))
            vop(lambda E, nsub=nsub: E.tensor_copy(
                rowt[0:1, 0:nsub * 128], P[5][0:1, 0:nsub * 128]),
                waits=[W(sT)], inc=True)
            vdrain()
            op("tensor",
               lambda E, dst_ps=dst_ps, nsub=nsub: E.matmul(
                   dst_ps[:, 0:nsub * 128], onesr[0:1, :],
                   rowt[0:1, 0:nsub * 128], start=True, stop=True),
               waits=[W(sV)], incs=[(sT, 1)])

        INTF = 0

        def emit_ffn(layer):
            dst = d_rs if layer == 1 else d_out
            src = d_x if layer == 1 else d_rs
            for q in range(NQ):
                for s in range(NSUB):
                    S = q * NSUB + s
                    xsl = xt[:, s * D:(s + 1) * D]
                    dma("sync", xsl, src[S * 128:(S + 1) * 128, :],
                        waits=[W(sT), W(sV), W(sS)])
                    ld_waits = [W(sD)]
                    ln_stats(xsl, ld_waits)
                    sop(lambda E, xsl=xsl: E.activation(
                        xn[:], xsl, AF.Identity, bias=stats[:, 48:49],
                        scale=stats[:, 47:48]), waits=[W(sV)])
                    sdrain()
                    vop(lambda E: E.reduce_max(
                        stats[:, 49:50], xn[:], axis=AXX,
                        apply_absolute_value=True), waits=[W(sS)])
                    vdrain()
                    vop(lambda E: E.tensor_scalar(
                        stats[:, 50:51], stats[:, 49:50], EPS, None, OP.max))
                    vdrain()
                    vop(lambda E: E.reciprocal(stats[:, 51:52],
                                               stats[:, 50:51]))
                    vdrain()
                    vop(lambda E: E.tensor_scalar(
                        stats[:, 52:53], stats[:, 51:52], 128.0, None,
                        OP.mult))
                    vop(lambda E, s=s: E.tensor_scalar(
                        stats[:, 20 + s:21 + s], stats[:, 50:51], INV128,
                        None, OP.mult), inc=True)
                    vdrain()
                    sop(lambda E: E.activation(
                        tq1[:], xn[:], AF.Identity, scale=stats[:, 52:53]),
                        waits=[W(sV)])
                    sdrain()
                    vop(lambda E: E.tensor_scalar(
                        tq2[:], tq1[:], M1, M1, OP.add, OP.subtract),
                        waits=[W(sS)])
                    vdrain()
                    vop(lambda E: E.tensor_scalar(
                        xn[:], tq2[:], M2, M2, OP.add, OP.subtract))
                    vdrain()
                    vop(lambda E: E.scalar_tensor_tensor(
                        tq1[:], tq2[:], G1M_AP, xn[:], OP.mult, OP.add))
                    vdrain()
                    vop(lambda E: E.tensor_scalar(
                        xqb[:], tq1[:], -128.0, 127.0, OP.max, OP.min),
                        inc=True)
                    vdrain()
                    vv = sV.v
                    for hc in range(16):
                        pb = PBa if hc < 8 else PBb
                        off = (hc % 8) * 128
                        op("tensor",
                           lambda E, pb=pb, off=off, hc=hc: E.transpose(
                               pb[:, off:off + 128],
                               xqb[:, hc * 128:(hc + 1) * 128], idb[:]),
                           waits=([W(sV, vv)] if hc == 0 else ()),
                           incs=([(sT, 1)] if hc in (7, 15) else ()))
                    tv = sT.v
                    vop(lambda E, s=s: E.tensor_copy(
                        xqT[:, s * 2048:s * 2048 + 1024], PBa[:]),
                        waits=[W(sT, tv - 1), W(sG)])
                    vop(lambda E, s=s: E.tensor_copy(
                        xqT[:, s * 2048 + 1024:s * 2048 + 2048], PBb[:]),
                        waits=[W(sT, tv)], inc=True)
                    vdrain()
                bcast_cols(20, NSUB, P[4], waits=[W(sV)])
                vop(lambda E: E.tensor_scalar(
                    sbt[:, 0:QT], P[4][:, 0:QT], SWG_AP, None, OP.mult),
                    waits=[W(sT)])
                vdrain()
                vop(lambda E: E.tensor_scalar(
                    sbt2[:, 0:QT], P[4][:, 0:QT], SWU_AP, None, OP.mult),
                    inc=True)
                vdrain()
                xqT_rhs = xqT[:].rearrange("p (s c) -> p s c", c=2048)
                mm_done = {}
                ev_done2 = {}
                for p in range(64):
                    db = p % 2
                    gsl = wsl[:, db * 4096:db * 4096 + 2048]
                    usl = wsl[:, db * 4096 + 2048:db * 4096 + 4096]
                    dsem = sDab[p % 2]
                    dw = ([W(sT, mm_done[p - 2])] if p >= 2 else []) + \
                        [W(dsem)]
                    dma("sync",
                        gsl.rearrange("p (hc i) -> p hc i", i=128),
                        d_wgt[p].rearrange("(hc hp) i -> hp hc i", hp=128),
                        waits=dw, sem=dsem)
                    dma("sync",
                        usl.rearrange("p (hc i) -> p hc i", i=128),
                        d_wut[p].rearrange("(hc hp) i -> hp hc i", hp=128),
                        sem=dsem)
                    ldv = dsem.v
                    pg = P[p % 2]
                    pu = P[2 + p % 2]
                    mmw = [W(dsem, ldv)]
                    if p < 2:
                        mmw.append(W(sV))
                    else:
                        mmw.append(W(sV, ev_done2[p - 2]))
                    for hc in range(16):
                        rhs = xqT_rhs[:, :, hc * 128:(hc + 1) * 128]
                        op("tensor",
                           lambda E, pg=pg, gsl=gsl, hc=hc, rhs=rhs:
                           E.matmul(pg[:, 0:QT],
                                    gsl[:, hc * 128:(hc + 1) * 128],
                                    rhs, start=(hc == 0), stop=(hc == 15)),
                           waits=(mmw if hc == 0 else ()))
                        op("tensor",
                           lambda E, pu=pu, usl=usl, hc=hc, rhs=rhs:
                           E.matmul(pu[:, 0:QT],
                                    usl[:, hc * 128:(hc + 1) * 128],
                                    rhs, start=(hc == 0), stop=(hc == 15)),
                           incs=([(sT, 1)] if hc == 15 else ()))
                    mm_done[p] = sT.v
                    gax = ga if db == 0 else gc
                    gbx = gb if db == 0 else gd
                    vop(lambda E, pg=pg, gax=gax: E.tensor_tensor(
                        gax[:, 0:QT], pg[:, 0:QT], sbt[:, 0:QT], OP.mult),
                        waits=[W(sT, mm_done[p])] +
                        ([W(sS)] if p >= 2 else []), inc=True)
                    vdrain()
                    sop(lambda E, gax=gax, gbx=gbx: E.activation(
                        gbx[:, 0:QT], gax[:, 0:QT], AF.Silu), waits=[W(sV)])
                    sdrain()
                    vop(lambda E, pu=pu, gax=gax: E.tensor_tensor(
                        gax[:, 0:QT], pu[:, 0:QT], sbt2[:, 0:QT], OP.mult),
                        waits=[W(sS)])
                    vdrain()
                    vop(lambda E, p=p, gax=gax, gbx=gbx: E.tensor_tensor(
                        arf(INTF + p * QT, QT), gbx[:, 0:QT], gax[:, 0:QT],
                        OP.mult), waits=[W(sS)], inc=True)
                    vdrain()
                    ev_done2[p] = sV.v
                ev_done = {}
                for p in range(64):
                    ph = P[4 + p % 2]
                    if p == 0:
                        w0 = [W(sV)]
                    elif p >= 2:
                        w0 = [W(sS, ev_done[p - 2])]
                    else:
                        w0 = []
                    op("tensor",
                       lambda E, ph=ph, p=p: E.matmul(
                           ph[:, 0:QT], h128[:], arf(INTF + p * QT, QT),
                           start=True, stop=True),
                       waits=w0, incs=[(sT, 1)])
                    tv = sT.v
                    sop(lambda E, ph=ph, p=p: E.copy(
                        arf(INTF + p * QT, QT), ph[:, 0:QT]),
                        waits=[W(sT, tv)])
                    ev_done[p] = sS.v
                first = True
                for hp in (1, 2, 4, 8, 16, 32):
                    view = AR[:, INTF:INTF + 64 * QT].rearrange(
                        "p (a two b) -> p a two b", two=2, b=hp * QT)
                    ev = view[:, :, 0, :]
                    od = view[:, :, 1, :]
                    vop(lambda E, ev=ev, od=od: E.tensor_tensor(
                        ev, ev, od, OP.add),
                        waits=([W(sS)] if first else ()))
                    first = False
                    vdrain()
                    vop(lambda E, ev=ev, od=od: E.scalar_tensor_tensor(
                        od, od, -2.0, ev, OP.mult, OP.add))
                    vdrain()
                vop(lambda E: E.tensor_copy(ga[:, 0:QT], arf(INTF, QT)))
                vop(lambda E: E.tensor_copy(gc[:, 0:QT], arf(INTF, QT)))
                vdrain()
                cur = 0
                for p in range(1, 64):
                    a, b = (ga, gb) if cur == 0 else (gb, ga)
                    c_, d_ = (gc, gd) if cur == 0 else (gd, gc)
                    vop(lambda E, a=a, b=b, p=p: E.tensor_tensor(
                        b[:, 0:QT], a[:, 0:QT], arf(INTF + p * QT, QT),
                        OP.max))
                    vop(lambda E, c_=c_, d_=d_, p=p: E.tensor_tensor(
                        d_[:, 0:QT], c_[:, 0:QT], arf(INTF + p * QT, QT),
                        OP.min))
                    vdrain()
                    cur ^= 1
                a = ga if cur == 0 else gb
                c_ = gc if cur == 0 else gd
                vop(lambda E, c_=c_: E.tensor_scalar(
                    sbt[:, 0:QT], c_[:, 0:QT], -1.0, None, OP.mult))
                vdrain()
                vop(lambda E, a=a: E.tensor_tensor(
                    sbt2[:, 0:QT], a[:, 0:QT], sbt[:, 0:QT], OP.max),
                    inc=True)
                vdrain()
                vv = sV.v
                for s4 in range(NSUB):
                    op("tensor",
                       lambda E, s4=s4: E.transpose(
                           P[4][:, s4 * 128:(s4 + 1) * 128],
                           sbt2[:, s4 * 128:(s4 + 1) * 128], idf[:]),
                       waits=([W(sV, vv)] if s4 == 0 else ()),
                       incs=([(sT, 1)] if s4 == NSUB - 1 else ()))
                for s4 in range(NSUB):
                    vop(lambda E, s4=s4: E.reduce_max(
                        stats[:, 24 + s4:25 + s4],
                        P[4][:, s4 * 128:(s4 + 1) * 128], axis=AXX),
                        waits=([W(sT)] if s4 == 0 else ()))
                vdrain()
                for s4 in range(NSUB):
                    vop(lambda E, s4=s4: E.tensor_scalar(
                        stats[:, 28 + s4:29 + s4], stats[:, 24 + s4:25 + s4],
                        HNORM, EPS, OP.mult, OP.max))
                vdrain()
                for s4 in range(NSUB):
                    vop(lambda E, s4=s4: E.reciprocal(
                        stats[:, 24 + s4:25 + s4],
                        stats[:, 28 + s4:29 + s4]))
                vdrain()
                for s4 in range(NSUB):
                    vop(lambda E, s4=s4: E.tensor_scalar(
                        stats[:, 32 + s4:33 + s4], stats[:, 24 + s4:25 + s4],
                        128.0, None, OP.mult))
                vdrain()
                for s4 in range(NSUB):
                    vop(lambda E, s4=s4: E.tensor_scalar(
                        stats[:, 36 + s4:37 + s4], stats[:, 32 + s4:33 + s4],
                        HNORM, None, OP.mult))
                    vop(lambda E, s4=s4: E.tensor_scalar(
                        stats[:, 56 + s4:57 + s4], stats[:, 28 + s4:29 + s4],
                        INV128, None, OP.mult), inc=(s4 == NSUB - 1))
                vdrain()
                bcast_cols(36, NSUB, P[4], waits=[W(sV)])
                vop(lambda E: E.tensor_copy(sbt[:, 0:QT], P[4][:, 0:QT]),
                    waits=[W(sT)], inc=True)
                vdrain()
                for p in range(64):
                    ch = arf(INTF + p * QT, QT)
                    aqv = arb(INTF + p * QT, QT)
                    vop(lambda E, ch=ch: E.tensor_tensor(
                        ga[:, 0:QT], ch, sbt[:, 0:QT], OP.mult))
                    vdrain()
                    vop(lambda E: E.tensor_scalar(
                        gb[:, 0:QT], ga[:, 0:QT], M1, M1, OP.add,
                        OP.subtract))
                    vdrain()
                    vop(lambda E: E.tensor_scalar(
                        gc[:, 0:QT], gb[:, 0:QT], M2, M2, OP.add,
                        OP.subtract))
                    vdrain()
                    vop(lambda E: E.scalar_tensor_tensor(
                        gd[:, 0:QT], gb[:, 0:QT], G1M_AP, gc[:, 0:QT],
                        OP.mult, OP.add))
                    vdrain()
                    vop(lambda E, aqv=aqv: E.tensor_scalar(
                        aqv, gd[:, 0:QT], -128.0, 127.0, OP.max, OP.min),
                        inc=True)
                    vdrain()
                for s4 in range(NSUB):
                    vop(lambda E, s4=s4: E.tensor_scalar(
                        stats[:, 60 + s4:61 + s4], stats[:, 56 + s4:57 + s4],
                        SWD_AP, 0.5, OP.mult, OP.mult), inc=(s4 == NSUB - 1))
                vdrain()
                dslab_done = {}
                PSD = P + [PBa[:].bitcast(F32), PBb[:].bitcast(F32)]
                for p in range(64):
                    db = p % 2
                    dsl = wdsl[:, db * 2048:(db + 1) * 2048]
                    dsem = sDcd[p % 2]
                    dw = ([W(sT, dslab_done[p - 2])] if p >= 2 else []) + \
                        [W(dsem)]
                    dma("sync", dsl, d_wdt[p], waits=dw, sem=dsem)
                    ldv = dsem.v
                    for oq in range(8):
                        o, s4 = oq // 2, oq % 2
                        aqt = arb(INTF + p * QT, QT)
                        w0 = ()
                        if oq == 0:
                            w0 = [W(dsem, ldv)]
                            if p == 0:
                                w0 = w0 + [W(sV), W(sS)]
                        op("tensor",
                           lambda E, o=o, s4=s4, oq=oq, dsl=dsl, aqt=aqt,
                           p=p, PSD=PSD: E.matmul(
                               PSD[oq][:, 0:512],
                               aqt[:, s4 * 128:(s4 + 1) * 128],
                               dsl[:, o * 512:(o + 1) * 512],
                               start=(p == 0), stop=(p == 63)),
                           waits=w0,
                           incs=([(sT, 1)] if oq == 7 else ()))
                    dslab_done[p] = sT.v
                tv = sT.v
                for oq in range(8):
                    o, s4 = oq // 2, oq % 2
                    S = q * NSUB + s4
                    oa = ga if oq % 2 == 0 else gb
                    ob = gc if oq % 2 == 0 else gd
                    sop(lambda E, oq=oq, s4=s4, oa=oa, PSD=PSD:
                        E.activation(
                            oa[:], PSD[oq][:, 0:512], AF.Identity,
                            scale=stats[:, 60 + s4:61 + s4]),
                        waits=(([W(sT, tv)] if oq == 0 else []) +
                               [W(sV)]))
                    sdrain()
                    sv = sS.v
                    vop(lambda E, s4=s4, o=o, oa=oa, ob=ob:
                        E.tensor_tensor(
                            ob[:],
                            xt[:, s4 * D + o * 512:s4 * D + o * 512
                               + 512], oa[:], OP.add),
                        waits=[W(sS, sv), W(sG)], inc=True)
                    vdrain()
                    dma("gpsimd",
                        dst[S * 128:(S + 1) * 128,
                            o * 512:(o + 1) * 512],
                        ob[:], waits=[W(sV)])

        emit_ffn(1)

        X1H = 0
        X1L = 8192
        WSF = 16384
        WHI = 18432
        WLO = 19456

        for r in range(16):
            dma("sync", tq2[:, 0:D], d_wout[r * 128:(r + 1) * 128, :],
                waits=[W(sV), W(sG)])
            ldv = sD.v
            vop(lambda E: E.tensor_copy(xqb[:, 0:D], tq2[:, 0:D]),
                waits=[W(sD, ldv), W(sG)])
            vdrain()
            vop(lambda E: E.tensor_tensor(
                tq1[:].bitcast(BF16)[:, 0:D], tq2[:, 0:D], xqb[:, 0:D],
                OP.subtract), inc=True)
            vdrain()
            dma("gpsimd", d_wo2[0, r * 128:(r + 1) * 128, :], xqb[:, 0:D],
                waits=[W(sV)])
            dma("gpsimd", d_wo2[1, r * 128:(r + 1) * 128, :],
                tq1[:].bitcast(BF16)[:, 0:D])

        for S in range(8):
            dma("sync", xt[:, 0:D], d_rs[S * 128:(S + 1) * 128, :],
                waits=[W(sV), W(sS), W(sT), W(sG)])
            ln_stats(xt[:, 0:D], [W(sD)])
            sop(lambda E: E.activation(
                xn[:], xt[:, 0:D], AF.Identity, bias=stats[:, 48:49],
                scale=stats[:, 47:48]), waits=[W(sV)])
            sdrain()
            vop(lambda E: E.tensor_copy(xqb[:], xn[:]), waits=[W(sS)])
            vdrain()
            lobuf = tq1[:].bitcast(BF16)[:, 0:D]
            vop(lambda E, lobuf=lobuf: E.tensor_tensor(
                lobuf, xn[:], xqb[:], OP.subtract), inc=True)
            vdrain()
            vv = sV.v
            for hl in range(2):
                srcb = xqb[:] if hl == 0 else lobuf
                base = X1H if hl == 0 else X1L
                for hc in range(16):
                    pb = PBa if hc < 8 else PBb
                    off = (hc % 8) * 128
                    op("tensor",
                       lambda E, pb=pb, off=off, hc=hc, srcb=srcb:
                       E.transpose(pb[:, off:off + 128],
                                   srcb[:, hc * 128:(hc + 1) * 128],
                                   idb[:]),
                       waits=([W(sV, vv)] if (hl == 0 and hc == 0)
                              else ([W(sV)] if hc == 0 else ())),
                       incs=([(sT, 1)] if hc in (7, 15) else ()))
                tv = sT.v
                for g, pb in ((0, PBa), (1, PBb)):
                    for k in range(8):
                        mc = g * 8 + k
                        vop(lambda E, pb=pb, k=k, mc=mc, base=base, S=S:
                            E.tensor_copy(
                                arb(base, 16384)[
                                    :, mc * 1024 + S * 128:
                                    mc * 1024 + S * 128 + 128],
                                pb[:, k * 128:(k + 1) * 128]),
                            waits=([W(sT, tv - 1 + g)] if k == 0 else ()),
                            inc=(k == 7 and g == 1))
                vdrain()

        wq_done = {}
        for oc in range(48):
            wsf = arf(WSF, 2048)
            dw = [W(sV)] if oc == 0 else ([W(sV, wq_done[oc - 1])]
                                          if oc >= 1 else [])
            dma("sync",
                wsf.rearrange("p (mc o) -> p mc o", o=128),
                d_wqkv[:, oc * 128:(oc + 1) * 128].rearrange(
                    "(mc mp) o -> mp mc o", mp=128),
                waits=dw)
            ldv = sD.v
            vop(lambda E, wsf=wsf: E.tensor_copy(arb(WHI, 2048), wsf),
                waits=[W(sD, ldv)])
            vdrain()
            vop(lambda E, wsf=wsf: E.tensor_tensor(
                arb(WLO, 2048), wsf, arb(WHI, 2048), OP.subtract),
                inc=True)
            vdrain()
            wq_done[oc] = sV.v
            vvw = sV.v
            pq = [P[oc % 2], P[2 + oc % 2]]
            for mc in range(16):
                whi_t = arb(WHI, 2048)[:, mc * 128:(mc + 1) * 128]
                wlo_t = arb(WLO, 2048)[:, mc * 128:(mc + 1) * 128]
                xhi = arb(X1H, 16384)[:, mc * 1024:(mc + 1) * 1024]
                xlo = arb(X1L, 16384)[:, mc * 1024:(mc + 1) * 1024]
                pairs = [(whi_t, xhi), (whi_t, xlo), (wlo_t, xhi)]
                for pi, (lhs, rhs) in enumerate(pairs):
                    st = (mc == 0 and pi == 0)
                    sp = (mc == 15 and pi == 2)
                    for half in range(2):
                        op("tensor",
                           lambda E, lhs=lhs, rhs=rhs, st=st, sp=sp,
                           half=half, pq=pq: E.matmul(
                               pq[half][:],
                               lhs, rhs[:, half * 512:(half + 1) * 512],
                               start=st, stop=sp),
                           waits=([W(sV, vvw)]
                                  if (st and half == 0) else ()),
                           incs=([(sT, 1)] if (sp and half == 1) else ()))
            tv = sT.v
            sop(lambda E, pq=pq, oc=oc: E.activation(
                tq2[:, 0:512], pq[0][:], AF.Identity,
                bias=bqf[:, oc:oc + 1]), waits=[W(sT, tv), W(sV)])
            sop(lambda E, pq=pq, oc=oc: E.activation(
                tq2[:, 512:1024], pq[1][:], AF.Identity,
                bias=bqf[:, oc:oc + 1]))
            sdrain()
            sv = sS.v
            for s8 in range(8):
                op("tensor",
                   lambda E, s8=s8: E.transpose(
                       P[4 + s8 % 2][:, (s8 // 2 % 2) * 128:
                                     (s8 // 2 % 2) * 128 + 128],
                       tq2[:, s8 * 128:(s8 + 1) * 128], idf[:]),
                   waits=([W(sS, sv)] if s8 == 0 else
                          ([W(sV)] if s8 >= 2 else ())),
                   incs=[(sT, 1)])
                tv2 = sT.v
                vop(lambda E, s8=s8: E.tensor_copy(
                    gc[:, 0:128] if s8 % 2 == 0 else gd[:, 0:128],
                    P[4 + s8 % 2][:, (s8 // 2 % 2) * 128:
                                  (s8 // 2 % 2) * 128 + 128]),
                    waits=[W(sT, tv2), W(sG)], inc=True)
                vdrain()
                dma("gpsimd",
                    d_qs[s8 * 128:(s8 + 1) * 128,
                         oc * 128:(oc + 1) * 128],
                    gc[:, 0:128] if s8 % 2 == 0 else gd[:, 0:128],
                    waits=[W(sV)])

        QV = 0
        SC = 6144
        PE = 6400
        DN = 6656
        ATN = 6720
        ATT = 8768
        ATH = 10816
        ATL = 11840
        WOS = 12864
        NPAIR = 256

        wo_done = {}
        for S in range(8):
            dma("sync", arf(QV, 6144), d_qs[S * 128:(S + 1) * 128, :],
                waits=[W(sT), W(sV), W(sS)])
            ldv = sD.v
            qall = arf(QV, 6144).rearrange("p (h g d) -> p h g d",
                                           g=3, d=128)
            mtv = arf(ATT, 2048).rearrange("p (h d) -> p h d", d=128)
            scv = arf(SC, 256).rearrange("p (h1 h2) -> p h1 h2", h2=16)
            vop(lambda E: E.memset(arf(SC, 256), 0.0),
                waits=[W(sD, ldv), W(sG)])
            vdrain()
            for h2 in range(16):
                nh = 16 - h2
                kb = arf(QV + h2 * 384 + 128, 128).unsqueeze(1) \
                    .broadcast_to([128, nh, 128])
                vop(lambda E, h2=h2, nh=nh, kb=kb: E.tensor_tensor(
                    mtv[:, h2:16, :], qall[:, h2:16, 0, :], kb, OP.mult))
                vdrain()
                vop(lambda E, h2=h2, nh=nh: E.tensor_reduce(
                    scv[:, h2:16, h2:h2 + 1], mtv[:, h2:16, :],
                    axis=AXX, op=OP.add))
                vdrain()
            op("vector", lambda E: E.drain(), incs=[(sV, 1)])
            sop(lambda E: E.activation(
                arf(PE, 256)[:, 0:NPAIR], arf(SC, 256)[:, 0:NPAIR],
                AF.Exp, bias=NEGC_AP, scale=ISQ), waits=[W(sV)])
            sdrain()
            for h1 in range(16):
                vop(lambda E, h1=h1: E.reduce_sum(
                    arf(DN, 32)[:, h1:h1 + 1],
                    arf(PE, 256)[:, h1 * 16:h1 * 16 + h1 + 1], axis=AXX),
                    waits=([W(sS)] if h1 == 0 else ()))
            vdrain()
            vop(lambda E: E.reciprocal(arf(DN, 32)[:, 16:32],
                                       arf(DN, 32)[:, 0:16]))
            vdrain()
            vop(lambda E: E.tensor_tensor(
                arf(DN, 64)[:, 32:48], arf(DN, 32)[:, 0:16],
                arf(DN, 32)[:, 16:32], OP.mult), inc=True)
            vdrain()
            vop(lambda E: E.tensor_copy(ga[:, 0:128],
                                        arf(QV + 256, 128)))
            vdrain()
            curv = 0
            for h2 in range(1, 16):
                a_, b_ = (ga, gb) if curv == 0 else (gb, ga)
                vop(lambda E, a_=a_, b_=b_, h2=h2: E.tensor_tensor(
                    b_[:, 0:128], a_[:, 0:128],
                    arf(QV + h2 * 384 + 256, 128), OP.add))
                vdrain()
                curv ^= 1
            vsb = ga if curv == 0 else gb
            for h1 in range(16):
                vop(lambda E, h1=h1, vsb=vsb: E.tensor_scalar(
                    arf(ATN + h1 * 128, 128), vsb[:, 0:128],
                    arf(DN, 64)[:, 32 + h1:33 + h1], None, OP.mult),
                    inc=(h1 == 15))
            vdrain()
            vv = sV.v
            for hc in range(16):
                op("tensor",
                   lambda E, hc=hc: E.transpose(
                       P[4 + (hc // 4) % 2][:, (hc % 4) * 128:
                                            (hc % 4) * 128 + 128],
                       arf(ATN + hc * 128, 128), idf[:]),
                   waits=([W(sV, vv)] if hc == 0 else
                          ([W(sV)] if hc % 4 == 0 and hc >= 8 else ())),
                   incs=([(sT, 1)] if hc % 4 == 3 else ()))
                if hc % 4 == 3:
                    tv2 = sT.v
                    g0 = hc - 3
                    vop(lambda E, g0=g0, hc=hc: E.tensor_copy(
                        arf(ATT + g0 * 128, 512),
                        P[4 + (hc // 4) % 2][:, 0:512]),
                        waits=[W(sT, tv2)], inc=True)
                    vdrain()
            vop(lambda E: E.tensor_copy(arb(ATH, 2048), arf(ATT, 2048)))
            vdrain()
            vop(lambda E: E.tensor_tensor(
                arb(ATL, 2048), arf(ATT, 2048), arb(ATH, 2048),
                OP.subtract), inc=True)
            vdrain()
            vva = sV.v
            for hd in range(16):
                db = hd % 2
                wsh = arb(WOS + db * 2048, 2048)
                wsl2 = arb(WOS + db * 2048 + 1024, 2048)
                key = S * 16 + hd
                dsem = sDef[hd % 2]
                dwx = ([W(sT, wo_done[key - 2])] if key >= 2 else []) + \
                    [W(dsem)]
                dma("sync", wsh, d_wo2[0, hd * 128:(hd + 1) * 128, :],
                    waits=dwx, sem=dsem)
                dma("sync", wsl2, d_wo2[1, hd * 128:(hd + 1) * 128, :],
                    sem=dsem)
                ldw = dsem.v
                ahi = arb(ATH, 2048)[:, hd * 128:(hd + 1) * 128]
                alo = arb(ATL, 2048)[:, hd * 128:(hd + 1) * 128]
                pairs = [(ahi, wsh), (ahi, wsl2), (alo, wsh)]
                for pi, (lhs, rhs) in enumerate(pairs):
                    st = (hd == 0 and pi == 0)
                    sp = (hd == 15 and pi == 2)
                    first = (pi == 0)
                    for quar in range(4):
                        op("tensor",
                           lambda E, lhs=lhs, rhs=rhs, st=st, sp=sp,
                           quar=quar: E.matmul(
                               P[quar][:], lhs,
                               rhs[:, quar * 512:(quar + 1) * 512],
                               start=st, stop=sp),
                           waits=([W(dsem, ldw), W(sV, vva)]
                                  if (first and quar == 0) else ()),
                           incs=([(sT, 1)]
                                 if (pi == 2 and quar == 3) else ()))
                wo_done[key] = sT.v
            tv = sT.v
            dma("sync", xt[:, 0:D], d_rs[S * 128:(S + 1) * 128, :])
            ldx = sD.v
            for quar in range(4):
                sop(lambda E, quar=quar: E.activation(
                    ga[:] if quar % 2 == 0 else gb[:], P[quar][:],
                    AF.Identity),
                    waits=(([W(sT, tv)] if quar == 0 else []) +
                           [W(sV), W(sG)]))
                sdrain()
                sv = sS.v
                vop(lambda E, quar=quar: E.tensor_tensor(
                    gc[:] if quar % 2 == 0 else gd[:],
                    ga[:] if quar % 2 == 0 else gb[:],
                    boutr[:, quar * 512:(quar + 1) * 512], OP.add),
                    waits=[W(sS, sv), W(sD, ldx), W(sG)])
                vdrain()
                vop(lambda E, quar=quar: E.tensor_tensor(
                    ga[:] if quar % 2 == 1 else gb[:],
                    gc[:] if quar % 2 == 0 else gd[:],
                    xt[:, quar * 512:(quar + 1) * 512], OP.add),
                    inc=True)
                vdrain()
                dma("gpsimd",
                    d_rs[S * 128:(S + 1) * 128,
                         quar * 512:(quar + 1) * 512],
                    ga[:] if quar % 2 == 1 else gb[:], waits=[W(sV)])

        emit_ffn(3)

        if debug:
            d_dbg = {}
            d_dbg["dbg_rs"] = nc.dram_tensor("dbg_rs", [TPC, D], F32,
                                             kind="ExternalOutput").ap()
            d_dbg["dbg_qs"] = nc.dram_tensor("dbg_qs", [TPC, 6144], F32,
                                             kind="ExternalOutput").ap()
            d_dbg["dbg_co"] = nc.dram_tensor("dbg_co", [128, 16], F32,
                                             kind="ExternalOutput").ap()
            d_dbg["dbg_wgt"] = nc.dram_tensor("dbg_wgt", [64, D, 128], BF16,
                                              kind="ExternalOutput").ap()
            dma("gpsimd", d_dbg["dbg_rs"], d_rs, waits=[W(sG), W(sD)])
            dma("gpsimd", d_dbg["dbg_qs"], d_qs)
            dma("gpsimd", d_dbg["dbg_co"], consts[:])
            dma("gpsimd", d_dbg["dbg_wgt"], d_wgt)

        op("gpsimd", lambda E: None, waits=[W(sG)])
        op("sync", lambda E: None, waits=[W(sD)])

        blk = es.enter_context(nc.Block())
        sch.emit(blk)
    return nc


def _consts():
    idf = np.eye(128, dtype=np.float32)
    idb = np.eye(128, dtype=ml_dtypes.bfloat16)
    h = np.array([[1.0]], dtype=np.float32)
    while h.shape[0] < 128:
        h = np.block([[h, h], [h, -h]])
    h128 = h.astype(np.float32)
    return idf, idb, h128


def _inputs_for_core(c, x, wqkv, bqkv, wout, bout, wg, wu, wd, consts):
    idf, idb, h128 = consts
    wqkv_f = np.ascontiguousarray(wqkv.reshape(D, 6144))
    wout_f = np.ascontiguousarray(wout.reshape(D, D))
    bq_f = np.ascontiguousarray(
        bqkv.reshape(6144).reshape(48, 128).T.astype(np.float32))
    return {
        "x": np.ascontiguousarray(x[c * TPC:(c + 1) * TPC, :]),
        "wg": wg, "wu": wu, "wd": wd,
        "wqkv": wqkv_f, "wout": wout_f,
        "bq_f": bq_f,
        "bout_r": np.ascontiguousarray(
            np.tile(bout[None, :], (128, 1)).astype(np.float32)),
        "ident_f": idf, "ident_b": idb, "h128": h128,
    }


def kernel(L0_attn_x, L0_attn_Wqkv, L0_attn_bqkv, L0_attn_Wout, L0_attn_bout,
           L0_ffn_w_gate, L0_ffn_w_up, L0_ffn_w_down, _debug=False):
    x = np.ascontiguousarray(
        np.asarray(L0_attn_x, dtype=np.float32).reshape(8192, D))
    wqkv = np.ascontiguousarray(np.asarray(L0_attn_Wqkv, dtype=np.float32))
    bqkv = np.ascontiguousarray(np.asarray(L0_attn_bqkv, dtype=np.float32))
    wout = np.ascontiguousarray(np.asarray(L0_attn_Wout, dtype=np.float32))
    bout = np.ascontiguousarray(np.asarray(L0_attn_bout, dtype=np.float32))
    wg = np.ascontiguousarray(np.asarray(L0_ffn_w_gate, dtype=np.float32))
    wu = np.ascontiguousarray(np.asarray(L0_ffn_w_up, dtype=np.float32))
    wd = np.ascontiguousarray(np.asarray(L0_ffn_w_down, dtype=np.float32))
    consts = _consts()
    nc = build_program(debug=_debug)
    in_maps = [_inputs_for_core(c, x, wqkv, bqkv, wout, bout, wg, wu, wd,
                                consts) for c in range(8)]
    res = run_bass_kernel_spmd(nc, in_maps, core_ids=list(range(8)))
    out = np.concatenate([res.results[c]["out"] for c in range(8)], axis=0)
    if _debug:
        return out.reshape(1, 8192, D).astype(np.float32), res.results
    return out.reshape(1, 8192, D).astype(np.float32)
